# version 01
# baseline (speedup 1.0000x reference)
import math
import numpy as np
from contextlib import ExitStack

import concourse.bass as bass
import concourse.mybir as mybir
from concourse.bass_utils import run_bass_kernel_spmd

F32 = mybir.dt.float32
BF16 = mybir.dt.bfloat16
AF = mybir.ActivationFunctionType
ALU = mybir.AluOpType
AX = mybir.AxisListType

NCORES = 8
T = 2048
D = 1024
DC = 8
NH = 16
HD = 64
HPG = 4
NGR = NH // HPG
SC = float(1.0 / math.sqrt(8.0))
EPS = 1e-5

FUSED = False



class Tok:
    __slots__ = ("eng", "idx")

    def __init__(self, eng, idx):
        self.eng = eng
        self.idx = idx


class KB:
    ENGS = ("T", "A", "V", "G", "S")

    def __init__(self, nc):
        self.nc = nc
        self.ops = {e: [] for e in self.ENGS}
        self.host = {}

    def op(self, eng, fn, waits=(), dma=False):
        wl = []
        for w in waits:
            if w is None:
                continue
            self.ops[w.eng][w.idx][3] = True
            wl.append(w)
        self.ops[eng].append([fn, wl, dma, dma, None])
        return Tok(eng, len(self.ops[eng]) - 1)

    def mm(self, out, lhsT, rhs, start, stop, waits=()):
        return self.op("T", lambda e: e.matmul(out, lhsT, rhs, start=start, stop=stop), waits)

    def transp(self, out, in_, ident, waits=()):
        return self.op("T", lambda e: e.transpose(out, in_, ident), waits)

    def act(self, out, in_, func, bias=0.0, scale=1.0, accum=None, waits=()):
        return self.op("A", lambda e: e.activation(out, in_, func, bias=bias, scale=scale, accum_out=accum), waits)

    def copyA(self, out, in_, waits=()):
        return self.op("A", lambda e: e.activation(out, in_, AF.Identity), waits)

    def tt(self, out, a, b, aluop, waits=()):
        return self.op("V", lambda e: e.tensor_tensor(out, a, b, aluop), waits)

    def ts(self, out, in0, s1, s2, op0, op1=ALU.bypass, waits=()):
        if s2 is None:
            return self.op("V", lambda e: e.tensor_scalar(out, in0, s1, None, op0), waits)
        return self.op("V", lambda e: e.tensor_scalar(out, in0, s1, s2, op0, op1), waits)

    def stt(self, out, in0, s, in1, op0, op1, waits=()):
        return self.op("V", lambda e: e.scalar_tensor_tensor(out, in0, s, in1, op0, op1), waits)

    def copyV(self, out, in_, waits=()):
        return self.op("V", lambda e: e.tensor_copy(out, in_), waits)

    def copyG(self, out, in_, waits=()):
        return self.op("G", lambda e: e.tensor_copy(out, in_), waits)

    def red(self, out, in_, aluop, negate=False, waits=()):
        return self.op("V", lambda e: e.tensor_reduce(out, in_, AX.X, aluop, negate=negate), waits)

    def recip(self, out, in_, waits=()):
        return self.op("V", lambda e: e.reciprocal(out, in_), waits)

    def msetV(self, ap, val, waits=()):
        return self.op("V", lambda e: e.memset(ap, val), waits)

    def msetG(self, ap, val, waits=()):
        return self.op("G", lambda e: e.memset(ap, val), waits)

    def dma(self, out, in_, waits=()):
        return self.op("S", lambda e: e.dma_start(out, in_), waits, dma=True)

    def barrier(self):
        toks = [Tok(e, len(self.ops[e]) - 1) for e in self.ENGS if self.ops[e]]
        for t in toks:
            self.ops[t.eng][t.idx][3] = True
        for e in self.ENGS:
            self.op(e, lambda eng: eng.nop(), waits=toks)

    def wait_only(self, eng, waits):
        return self.op(eng, None, waits=waits)

    def din(self, name, arr, percore=False):
        h = self.nc.dram_tensor(name, list(arr.shape), F32, kind="ExternalInput")
        self.host[name] = ("percore" if percore else "shared", arr)
        return h

    def finalize(self, sems):
        for e in self.ENGS:
            c = 0
            for rec in self.ops[e]:
                if rec[3]:
                    c += 16 if rec[2] else 1
                    rec[4] = c

        engmap = {"T": "tensor", "A": "scalar", "V": "vector", "G": "gpsimd", "S": "sync"}
        with self.nc.Block() as block:
            for e in self.ENGS:
                if not self.ops[e]:
                    continue

                def mk(eng_key):
                    def f(eng):
                        for rec in self.ops[eng_key]:
                            wa = {}
                            for t in rec[1]:
                                v = self.ops[t.eng][t.idx][4]
                                assert v is not None
                                if wa.get(t.eng, 0) < v:
                                    wa[t.eng] = v
                            for s, v in sorted(wa.items()):
                                eng.wait_ge(sems[s], v)
                            if rec[0] is not None:
                                ins = rec[0](eng)
                                if rec[3]:
                                    ins.then_inc(sems[eng_key], 16 if rec[2] else 1)
                            else:
                                assert not rec[3]
                    return f

                getattr(block, engmap[e])(mk(e))

    def n_ops(self):
        return {e: len(self.ops[e]) for e in self.ENGS}



def _c(a):
    return np.ascontiguousarray(a, dtype=np.float32)


def _arrange(inp):
    A = {}
    x = np.asarray(inp["L0_attn_data"], dtype=np.float32)
    A["x"] = _c(x.transpose(0, 2, 1).reshape(NCORES, DC, 128, T)
                .transpose(0, 2, 1, 3)).reshape(NCORES, 128, DC * T)

    def arr_attn(Wqkv, b_qkv, b_out):
        d = {}
        wqk = Wqkv[:, :2048].reshape(DC, 128, 2, 8, 128)
        d["wqk"] = _c(wqk.transpose(2, 3, 1, 0, 4)).reshape(2, 8, 128, DC * 128)
        wv = Wqkv[:, 2048:].reshape(DC, 128, NGR, HPG * HD)
        d["wv"] = _c(wv.transpose(2, 1, 0, 3)).reshape(NGR, 128, DC * HPG * HD)
        if b_qkv is not None:
            d["bqkv"] = _c(np.asarray(b_qkv)[:2048].reshape(16, 2, 64)
                           .transpose(1, 2, 0)).reshape(64, 32)
            d["bv"] = _c(np.asarray(b_qkv)[2048:].reshape(1, 1024))
            d["bout"] = _c(np.asarray(b_out).reshape(8, 128).T)
        return d

    l0w = np.asarray(inp["L0_attn_W_qkv"], dtype=np.float32)
    a0 = arr_attn(l0w, inp["L0_attn_b_qkv"], inp["L0_attn_b_out"])
    a0["wout"] = _c(np.asarray(inp["L0_attn_W_out"], dtype=np.float32).reshape(16, 64, 1024))
    A["a0"] = a0

    wq = np.asarray(inp["L1_attn_W_q"], dtype=np.float32).transpose(1, 0, 2).reshape(D, D)
    wk = np.asarray(inp["L1_attn_W_k"], dtype=np.float32).transpose(1, 0, 2).reshape(D, D)
    wv1 = np.asarray(inp["L1_attn_W_v"], dtype=np.float32).transpose(1, 0, 2).reshape(D, D)
    l1w = np.concatenate([wq, wk, wv1], axis=1)
    a1 = arr_attn(l1w, None, None)
    a1["wout"] = _c(np.asarray(inp["L1_attn_W_o"], dtype=np.float32).reshape(16, 64, 1024))
    A["a1"] = a1

    f0 = {}
    gW = np.asarray(inp["L0_ffn_gate_W"], dtype=np.float32)
    lA = np.asarray(inp["L0_ffn_lora_A"], dtype=np.float32)
    ga = np.concatenate([gW.T.reshape(DC, 128, 8),
                         _c(lA.reshape(32, D).T).reshape(DC, 128, 32)], axis=2)
    f0["ga"] = _c(ga.transpose(1, 0, 2)).reshape(128, DC * 40)
    f0["gb"] = _c(np.asarray(inp["L0_ffn_gate_b"]).reshape(1, 8))
    f0["ball"] = _c(np.asarray(inp["L0_ffn_lora_B"], dtype=np.float32).transpose(0, 2, 1).reshape(32, 4096))
    wi = np.asarray(inp["L0_ffn_wi_W"], dtype=np.float32)
    f0["wi"] = _c(_c(wi.T).reshape(DC, 128, 32, 128).transpose(2, 1, 0, 3)).reshape(32, 128, DC * 128)
    f0["wib"] = _c(np.asarray(inp["L0_ffn_wi_b"]).reshape(32, 128).T)
    wo = np.asarray(inp["L0_ffn_wo_W"], dtype=np.float32)
    f0["wo"] = _c(wo.reshape(8, 128, 32, 128).transpose(0, 3, 2, 1)).reshape(8, 128, 32 * 128)
    f0["wob"] = _c(np.asarray(inp["L0_ffn_wo_b"]).reshape(8, 128).T)
    A["f0"] = f0

    f1 = {}
    cls = np.asarray(inp["L1_ffn_classifier_weight"], dtype=np.float32)
    f1["cls"] = _c(_c(cls.T).reshape(DC, 128, 40).transpose(1, 0, 2)).reshape(128, DC * 40)
    f1["eb"] = _c(np.asarray(inp["L1_ffn_e_score_correction_bias"]).reshape(1, 40))
    gu = np.asarray(inp["L1_ffn_gate_up_proj"], dtype=np.float32)[:32]
    f1["wgu"] = _c(gu.transpose(0, 2, 1).reshape(32, DC, 128, 1024)
                   .transpose(0, 2, 1, 3)).reshape(32, 128, DC * 1024)
    dn = np.asarray(inp["L1_ffn_down_proj"], dtype=np.float32)
    f1["wd"] = _c(dn.transpose(0, 2, 1).reshape(32, 4, 128, 1024)
                  .transpose(0, 2, 1, 3)).reshape(32, 128, 4 * 1024)
    A["f1"] = f1
    return A



def alloc(st, nc, name, shape, dtype, side=None):
    if side is None:
        return st.enter_context(nc.sbuf_tensor(name, shape, dtype))
    return st.enter_context(nc.sbuf_tensor(name, shape, dtype, side=side))


def emit_consts(k, B, st):
    nc = k.nc
    B["identf"] = alloc(st, nc, "identf", [128, 128], F32)
    B["iscr"] = alloc(st, nc, "iscr", [128, 128], F32)
    B["cm"] = alloc(st, nc, "cm", [128, 128], BF16)
    B["ones_bf"] = alloc(st, nc, "ones_bf", [128, 1], BF16)
    B["onesrow_bf"] = alloc(st, nc, "onesrow_bf", [1, 128], BF16)
    B["onesrow_f"] = alloc(st, nc, "onesrow_f", [1, 128], F32)
    B["epsc"] = alloc(st, nc, "epsc", [128, 1], F32)
    B["ones64r"] = alloc(st, nc, "ones64r", [65, 64], F32)
    t1 = k.op("G", lambda e: e.iota(B["iscr"][:], [[1, 128]], channel_multiplier=-1,
                                    allow_small_or_imprecise_dtypes=True))
    t2 = k.op("G", lambda e: e.tensor_scalar(B["identf"][:], B["iscr"][:], 0.0, None, ALU.is_equal),
              waits=[t1])
    B["t_ident"] = t2
    k.msetG(B["cm"][:], 1.0 / 1024.0)
    k.msetG(B["ones_bf"][:], 1.0)
    k.msetG(B["onesrow_bf"][:], 1.0)
    k.msetG(B["onesrow_f"][:], 1.0)
    k.msetG(B["ones64r"][:], 1.0)
    B["t_consts"] = k.msetG(B["epsc"][:], EPS)


def emit_load_x(k, B, xarr):
    xin = k.din("x", xarr[0], percore=True)
    k.host["x"] = ("percore", xarr)
    tl = k.dma(B["X"][:], xin[:])
    B["tX"] = tl
    B["tXb"] = k.copyV(B["Xb"][:], B["X"][:], waits=[tl])
    B["tXb_free"] = None


def emit_store_x(k, B):
    nc = k.nc
    xout = nc.dram_tensor("xout", [128, DC * T], F32, kind="ExternalOutput")
    td = k.dma(xout[:], B["X"][:], waits=[B["tX"], B["tXb"]])
    k.wait_only("S", [td])


def emit_attn(k, B, st, W, has_bias, pre):
    nc = k.nc
    ps_a, ps_b = B["ps_a"], B["ps_b"]
    Xb, X = B["Xb"], B["X"]

    qa = alloc(st, nc, pre + "qa", [65, T], BF16)
    qb = alloc(st, nc, pre + "qb", [65, T], BF16)
    ka = alloc(st, nc, pre + "ka", [65, T], BF16)
    kb = alloc(st, nc, pre + "kb", [65, T], BF16)
    vt = alloc(st, nc, pre + "vt", [128, 16 * HPG * 65], BF16)
    av = alloc(st, nc, pre + "av", [64, HPG * T], BF16)
    pt = [alloc(st, nc, pre + "pt0", [128, T], BF16),
          alloc(st, nc, pre + "pt1", [128, T], BF16)]
    wst = alloc(st, nc, pre + "wst", [128, 1024], F32)
    wqb = alloc(st, nc, pre + "wqb", [128, 1024], BF16)
    wkb = alloc(st, nc, pre + "wkb", [128, 1024], BF16)
    wvb = alloc(st, nc, pre + "wvb", [128, DC * HPG * HD], BF16)
    wob = alloc(st, nc, pre + "wob", [64, HPG * 1024], BF16)
    sq = alloc(st, nc, pre + "sq", [64, T], BF16)
    scf = alloc(st, nc, pre + "scf", [65, T], F32)
    scr = alloc(st, nc, pre + "scr", [1, 8], F32)

    wqk_h = k.din(pre + "wqk", W["wqk"])
    wv_h = k.din(pre + "wv", W["wv"])
    wout_h = k.din(pre + "wout", W["wout"])
    if has_bias:
        bqs = alloc(st, nc, pre + "bqs", [64, 32], F32)
        bvst = alloc(st, nc, pre + "bvst", [1, 1024], F32)
        bvb = alloc(st, nc, pre + "bvb", [1, 1024], BF16)
        bob = alloc(st, nc, pre + "bob", [128, 8], F32)
        tb = k.dma(bqs[:], k.din(pre + "bqkv", W["bqkv"])[:])
        t_bqs = k.ts(bqs[:], bqs[:], SC, None, ALU.mult, waits=[tb])
        tb2 = k.dma(bvst[:], k.din(pre + "bv", W["bv"])[:])
        t_bvb = k.copyG(bvb[:], bvst[:], waits=[tb2])
        t_bob = k.dma(bob[:], k.din(pre + "bout", W["bout"])[:])
    else:
        t_bqs = t_bvb = t_bob = None

    t_ka1 = k.msetG(ka[64:65, :], 1.0)
    t_kb1 = k.msetG(kb[64:65, :], 1.0)

    tXb = B["tXb"]
    t_prev_opmm = [None] * 8
    t_evac_x = B["tX"]

    for g in range(NGR):
        tw = []
        for q4 in range(2):
            sl = slice(q4 * 1024, (q4 + 1) * 1024)
            td = k.dma(wst[:, :], wv_h[g, :, sl], waits=[B.get("t_wst_free")])
            tc = k.copyG(wvb[:, sl], wst[:, :],
                         waits=[td, B.get("t_wvb_free") if q4 == 0 else None])
            B["t_wst_free"] = tc
            tw.append(tc)
        t_vp_evac = [None] * 4
        t_vt_set = k.msetG(vt[:], 1.0, waits=[B.get("t_vt_free")])
        t_vt = []
        for tt in range(16):
            slot = tt % 4
            pb = ps_b[:, slot * 512: slot * 512 + 256]
            mmw = [tXb] + tw + [t_vp_evac[slot], B.get("t_psb_free")]
            tm = None
            for dcx in range(DC):
                tm = k.mm(pb, Xb[:, dcx * T + tt * 128: dcx * T + (tt + 1) * 128],
                          wvb[:, dcx * 256:(dcx + 1) * 256],
                          start=(dcx == 0), stop=(not has_bias and dcx == DC - 1),
                          waits=(mmw if dcx == 0 else []))
            if has_bias:
                tm = k.mm(pb, B["onesrow_bf"][0:1, 0:128],
                          bvb[0:1, g * 256:(g + 1) * 256], start=False, stop=True,
                          waits=[t_bvb] if tt == 0 else [])
            dst = vt[:, tt * (HPG * 65):(tt + 1) * (HPG * 65)] \
                .rearrange("p (h e) -> p h e", h=HPG)[:, :, 0:HD]
            src = pb.rearrange("p (h e) -> p h e", h=HPG)
            te = k.copyV(dst, src, waits=[tm, t_vt_set])
            t_vp_evac[slot] = te
            t_vt.append(te)
        t_vt_last = t_vt[-1]
        B["t_psb_free"] = t_vt_last

        for pr in range(HPG // 2):
            c = g * (HPG // 2) + pr
            tdq = k.dma(wst[:, :], wqk_h[0, c, :, :], waits=[B.get("t_wst_free")])
            tcq = k.copyG(wqb[:, :], wst[:, :], waits=[tdq, B.get("t_wqb_free")])
            tdk = k.dma(wst[:, :], wqk_h[1, c, :, :], waits=[tcq])
            tck = k.copyG(wkb[:, :], wst[:, :], waits=[tdk, B.get("t_wkb_free")])
            B["t_wst_free"] = tck

            t_ev = []
            for s, wb, tiles in ((0, wqb, (qa, qb)), (1, wkb, (ka, kb))):
                tcast = tcq if s == 0 else tck
                for tt in range(4):
                    for hl in (0, 1):
                        pa = ps_a[0:64, (tt % 2) * 1024 + hl * 512:
                                  (tt % 2) * 1024 + (hl + 1) * 512]
                        w0 = [tXb, tcast, B.get("t_psa_q%d%d" % (tt % 2, hl))]
                        tm = None
                        for dcx in range(DC):
                            tm = k.mm(pa,
                                      wb[:, dcx * 128 + hl * 64: dcx * 128 + hl * 64 + 64],
                                      Xb[:, dcx * T + tt * 512: dcx * T + (tt + 1) * 512],
                                      start=(dcx == 0), stop=(dcx == DC - 1),
                                      waits=(w0 if dcx == 0 else []))
                        if has_bias:
                            bias = bqs[0:64, (s * 8 + c) + hl * 16: (s * 8 + c) + hl * 16 + 1]
                        else:
                            bias = 0.0
                        ev = k.act(tiles[hl][0:64, tt * 512:(tt + 1) * 512], pa,
                                   AF.Identity, bias=bias, scale=SC,
                                   waits=[tm] + ([t_bqs] if has_bias else []))
                        t_ev.append(ev)
                        B["t_psa_q%d%d" % (tt % 2, hl)] = ev
            t_proj = t_ev[-1]

            t_qrow = {}
            for hl, (qt, kt) in enumerate(((qa, ka), (qb, kb))):
                tsq = k.tt(sq[:, :], kt[0:64, :], kt[0:64, :], ALU.mult,
                           waits=[t_proj, B.get("t_sq_free")])
                tr = []
                for j in range(4):
                    tr.append(k.mm(ps_b[0:1, j * 512:(j + 1) * 512],
                                   B["ones_bf"][0:64, 0:1], sq[:, j * 512:(j + 1) * 512],
                                   start=True, stop=True,
                                   waits=[tsq, B.get("t_psbrow_free"),
                                          B.get("t_psb_free"), B["t_consts"]]))
                tkmx = k.red(scr[0:1, 0:1], ps_b[0:1, 0:2048], ALU.max, waits=tr)
                tsq2 = k.tt(sq[:, :], qt[0:64, :], qt[0:64, :], ALU.mult,
                            waits=[t_proj] + tr)
                tq = []
                for j in range(4):
                    tq.append(k.mm(ps_b[0:1, j * 512:(j + 1) * 512],
                                   B["ones_bf"][0:64, 0:1], sq[:, j * 512:(j + 1) * 512],
                                   start=True, stop=True, waits=[tsq2, tkmx]))
                trow = k.ts(sq[0:1, :], ps_b[0:1, 0:2048], scr[0:1, 0:1], -0.5,
                            ALU.add, ALU.mult, waits=tq + [tkmx])
                tmv = k.dma(qt[64:65, :], sq[0:1, :], waits=[trow])
                B["t_sq_free"] = tmv
                t_qrow[hl] = tmv
                B["t_psbrow_free"] = trow

            for hl, (qt, kt) in enumerate(((qa, ka), (qb, kb))):
                hh = pr * 2 + hl
                t_exp = [None] * 16
                t_avmm = [None] * 16
                for kc in range(16):
                    stw = [t_qrow[hl], B.get("t_psa_free"), t_ka1, t_kb1] if kc == 0 \
                        else [t_exp[kc - 1]]
                    tm = None
                    for j in range(4):
                        tm = k.mm(ps_a[:, j * 512:(j + 1) * 512],
                                  kt[:, kc * 128:(kc + 1) * 128],
                                  qt[:, j * 512:(j + 1) * 512],
                                  start=True, stop=True,
                                  waits=(stw if j == 0 else []))
                    ew = [tm, t_avmm[kc - 2] if kc >= 2 else None]
                    if kc < 2:
                        ew.append(B.get("t_pt_free%d" % (kc % 2)))
                    te = k.act(pt[kc % 2][:, :], ps_a[:, :], AF.Exp, waits=ew)
                    t_exp[kc] = te
                    avw = [te]
                    if kc == 0:
                        avw += [B.get("t_psb_free"), B.get("t_psbrow_free"),
                                t_vt_last, t_qrow[1]]
                    ta = None
                    for j in range(4):
                        ta = k.mm(ps_b[0:65, j * 512:(j + 1) * 512],
                                  vt[:, (kc * HPG + hh) * 65:(kc * HPG + hh) * 65 + 65],
                                  pt[kc % 2][:, j * 512:(j + 1) * 512],
                                  start=(kc == 0), stop=(kc == 15),
                                  waits=(avw if j == 0 else []))
                    t_avmm[kc] = ta
                B["t_pt_free0"] = t_avmm[14]
                B["t_pt_free1"] = t_avmm[15]

                trc = k.recip(scf[64:65, :], ps_b[64:65, 0:2048], waits=[t_avmm[15]])
                tlb = []
                for j in range(4):
                    tlb.append(k.mm(ps_a[0:64, j * 512:(j + 1) * 512],
                                    B["ones64r"][64:65, 0:64],
                                    scf[64:65, j * 512:(j + 1) * 512],
                                    start=True, stop=True,
                                    waits=[trc, t_exp[15] if j == 0 else None,
                                           B["t_consts"] if j == 0 else None]))
                tlbc = k.copyV(scf[0:64, :], ps_a[0:64, 0:2048], waits=tlb)
                tev = k.stt(av[0:64, hh * T:(hh + 1) * T],
                            ps_b[0:64, 0:2048], 1.0, scf[0:64, :], ALU.mult, ALU.mult,
                            waits=[tlbc, t_avmm[15], B.get("t_av_free")])
                B["t_psa_free"] = tlbc
                B["t_psb_free"] = tev
                B["t_av_evac%d" % hh] = tev
            for key in ("t_psa_q00", "t_psa_q01", "t_psa_q10", "t_psa_q11"):
                B[key] = B["t_psa_free"]

        tow = []
        for hh in range(HPG):
            h_glob = g * HPG + hh
            td = k.dma(wst[0:64, :], wout_h[h_glob, :, :], waits=[B.get("t_wst_free")])
            tc = k.copyG(wob[0:64, hh * 1024:(hh + 1) * 1024], wst[0:64, :],
                         waits=[td, B.get("t_wob_free") if hh == 0 else None])
            B["t_wst_free"] = tc
            tow.append(tc)
        t_av_ready = [B["t_av_evac%d" % hh] for hh in range(HPG)]
        t_op_ev = []
        for tt in range(4):
            for oc in range(8):
                idx = (tt * 8 + oc) % 8
                ps = (ps_a if idx < 4 else ps_b)[0:128, (idx % 4) * 512:((idx % 4) + 1) * 512]
                w0 = tow + t_av_ready + [t_prev_opmm[idx],
                                         B.get("t_psa_free") if idx < 4 else B.get("t_psb_free")]
                tm = None
                for hh in range(HPG):
                    tm = k.mm(ps, wob[0:64, hh * 1024 + oc * 128: hh * 1024 + (oc + 1) * 128],
                              av[0:64, hh * T + tt * 512: hh * T + (tt + 1) * 512],
                              start=(hh == 0), stop=(hh == HPG - 1),
                              waits=(w0 if hh == 0 else []))
                if has_bias and g == 0:
                    b = bob[:, oc:oc + 1]
                    bw = [t_bob]
                else:
                    b = 0.0
                    bw = []
                te = k.stt(X[:, oc * T + tt * 512: oc * T + (tt + 1) * 512],
                           ps, b, X[:, oc * T + tt * 512: oc * T + (tt + 1) * 512],
                           ALU.add, ALU.add, waits=[tm, t_evac_x] + bw)
                t_prev_opmm[idx] = te
                t_op_ev.append(te)
        B["t_psa_free"] = t_op_ev[-1]
        B["t_psb_free"] = t_op_ev[-1]
        B["t_av_free"] = t_op_ev[-1]
        B["t_vt_free"] = t_op_ev[-1]
        B["t_wob_free"] = t_op_ev[-1]
        B["t_wqb_free"] = t_op_ev[-1]
        B["t_wkb_free"] = t_op_ev[-1]
        B["t_wvb_free"] = t_op_ev[-1]
        for key in ("t_psa_q00", "t_psa_q01", "t_psa_q10", "t_psa_q11"):
            B[key] = t_op_ev[-1]
        t_evac_x = t_op_ev[-1]

    B["tX"] = t_evac_x
    B["tXb_free"] = Tok("T", len(k.ops["T"]) - 1)



def emit_ln(k, B, st_scratch):
    mean_sb, m2v, sq_sb = st_scratch
    X, Xb = B["X"], B["Xb"]
    ps_a = B["ps_a"]
    t_out = None
    t_ps_rd = None
    for th in range(2):
        toff = th * 1024
        tmean_mm = None
        tsq_mm = None
        tcasts = []
        for dcx in range(DC):
            xs = X[:, dcx * T + toff: dcx * T + toff + 1024]
            xbs = Xb[:, dcx * T + toff: dcx * T + toff + 1024]
            w0 = [B["tX"]]
            if dcx == 0:
                w0.append(B.get("tXb_free"))
                w0.append(B.get("t_psa_free") if th == 0 else t_ps_rd)
            tc = k.copyV(xbs, xs, waits=w0)
            tcasts.append(tc)
            tq = k.tt(sq_sb[:], xbs, xbs, ALU.mult, waits=[tc, tsq_mm])
            for j in range(2):
                tmean_mm = k.mm(ps_a[:, j * 512:(j + 1) * 512], B["cm"][:],
                                Xb[:, dcx * T + toff + j * 512: dcx * T + toff + (j + 1) * 512],
                                start=(dcx == 0), stop=(dcx == DC - 1),
                                waits=(w0 + [tc]) if (j == 0 and dcx == 0)
                                else ([tc] if j == 0 else []))
            for j in range(2):
                tsq_mm = k.mm(ps_a[:, 1024 + j * 512: 1024 + (j + 1) * 512], B["cm"][:],
                              sq_sb[:, j * 512:(j + 1) * 512],
                              start=(dcx == 0), stop=(dcx == DC - 1),
                              waits=[tq] if j == 0 else [])
        tmean = k.copyV(mean_sb[:], ps_a[:, 0:1024], waits=[tmean_mm])
        tm2 = k.tt(m2v[:], mean_sb[:], mean_sb[:], ALU.mult, waits=[tmean])
        tvar = k.tt(m2v[:], ps_a[:, 1024:2048], m2v[:], ALU.subtract, waits=[tm2, tsq_mm])
        tln = k.act(m2v[:], m2v[:], AF.Ln, bias=B["epsc"][:], waits=[tvar, B["t_consts"]])
        trs = k.act(m2v[:], m2v[:], AF.Exp, scale=-0.5, waits=[tln])
        tlast = None
        for dcx in range(DC):
            xs = X[:, dcx * T + toff: dcx * T + toff + 1024]
            xbs = Xb[:, dcx * T + toff: dcx * T + toff + 1024]
            t1 = k.tt(xs, xs, mean_sb[:], ALU.subtract, waits=[tmean, tcasts[dcx]])
            t2 = k.tt(xs, xs, m2v[:], ALU.mult, waits=[t1, trs])
            t3 = k.copyV(xbs, xs, waits=[t2])
            tlast = t3
        t_out = tlast
        t_ps_rd = tvar
        B["t_psa_free"] = tvar
    B["tX"] = t_out
    B["tXb"] = t_out
    B["tXb_free"] = None


def emit_ffn0(k, B, st, W):
    nc = k.nc
    ps_a, ps_b = B["ps_a"], B["ps_b"]
    X, Xb = B["X"], B["Xb"]

    gac = alloc(st, nc, "f0gac", [128, DC * 40], BF16)
    gbbc = alloc(st, nc, "f0gbbc", [128, 8], F32)
    gbst = alloc(st, nc, "f0gbst", [1, 8], F32)
    ballb = alloc(st, nc, "f0ballb", [32, 4096], BF16)
    wib = alloc(st, nc, "f0wib", [128, 32], F32)
    wob = alloc(st, nc, "f0wob", [128, 8], F32)
    lgc = alloc(st, nc, "f0lgc", [128, 40], F32)
    mx = alloc(st, nc, "f0mx", [128, 1], F32)
    msk = alloc(st, nc, "f0msk", [128, 8], F32)
    chat = alloc(st, nc, "f0chat", [128, 32], F32)
    chT = alloc(st, nc, "f0chT", [32, T], BF16)
    wst = [alloc(st, nc, "f0wst0", [128, 1024], F32),
           alloc(st, nc, "f0wst1", [128, 1024], F32)]
    wibt = [alloc(st, nc, "f0wib0", [128, 1024], BF16),
            alloc(st, nc, "f0wib1", [128, 1024], BF16)]
    wost = alloc(st, nc, "f0wost", [128, 2048], F32)
    wocb = [alloc(st, nc, "f0woc0", [128, 4096], BF16),
            alloc(st, nc, "f0woc1", [128, 4096], BF16)]
    inter = alloc(st, nc, "f0inter", [128, 32 * 512], BF16)

    ga_h = k.din("f0ga", W["ga"])
    gb_h = k.din("f0gb", W["gb"])
    ball_h = k.din("f0ball", W["ball"])
    wi_h = k.din("f0wi", W["wi"])
    wib_h = k.din("f0wib_in", W["wib"])
    wo_h = k.din("f0wo", W["wo"])
    wob_h = k.din("f0wob_in", W["wob"])

    td = k.dma(wst[0][:, 0:DC * 40], ga_h[:])
    tgac = k.copyG(gac[:], wst[0][:, 0:DC * 40], waits=[td])
    tprev = tgac
    tball = None
    for q4 in range(4):
        td = k.dma(wst[1][0:32, :], ball_h[0:32, q4 * 1024:(q4 + 1) * 1024],
                   waits=[tball])
        tball = k.copyG(ballb[0:32, q4 * 1024:(q4 + 1) * 1024], wst[1][0:32, :],
                        waits=[td])
    tgb = k.dma(gbst[:], gb_h[:])
    twib = k.dma(wib[:], wib_h[:])
    twob = k.dma(wob[:], wob_h[:])
    tgm = k.mm(ps_b[:, 1024:1032], B["onesrow_f"][0:1, :], gbst[0:1, :],
               start=True, stop=True, waits=[tgb, B["t_consts"], B.get("t_psb_av_free")])
    tgbc = k.copyV(gbbc[:], ps_b[:, 1024:1032], waits=[tgm])

    t_tr_prev = None
    t_lg_prev = None
    tchT = []
    for ts_ in range(16):
        w0 = [B["tXb"], tgac, t_lg_prev]
        tm = None
        for dcx in range(DC):
            tm = k.mm(ps_b[:, 0:40], Xb[:, dcx * T + ts_ * 128: dcx * T + (ts_ + 1) * 128],
                      gac[:, dcx * 40:(dcx + 1) * 40], start=(dcx == 0), stop=(dcx == DC - 1),
                      waits=(w0 if dcx == 0 else []))
        tcp = k.copyV(lgc[:], ps_b[:, 0:40], waits=[tm, tgbc])
        t_lg_prev = tcp
        tga = k.tt(lgc[:, 0:8], lgc[:, 0:8], gbbc[:], ALU.add, waits=[tcp])
        tmx = k.red(mx[:], lgc[:, 0:8], ALU.max, waits=[tga])
        toh = k.ts(msk[:], lgc[:, 0:8], mx[:], None, ALU.is_equal, waits=[tmx])
        tch = []
        for r in range(4):
            tch.append(k.tt(chat[:, r:32:4], lgc[:, 8 + r:40:4], msk[:], ALU.mult,
                            waits=[toh, t_tr_prev]))
        ttr = k.transp(ps_b[0:32, 512:640], chat[:], B["identf"][:],
                       waits=tch + [B["t_ident"], t_tr_prev])
        t_tr_prev = ttr
        tcc = k.copyV(chT[0:32, ts_ * 128:(ts_ + 1) * 128], ps_b[0:32, 512:640], waits=[ttr])
        t_tr_prev = tcc
        tchT.append(tcc)

    t_wi_cast_free = [None, None]
    t_wi_mm_done = [None, None]
    t_inter_ev = [None] * 32
    t_x_ev = B["tX"]
    t_wocb_mm = [None, None]
    t_slot_ev = [None] * 4
    t_part1_done = t_tr_prev
    for tt in range(4):
        for fc in range(32):
            p = fc % 2
            td = k.dma(wst[p][:], wi_h[fc, :, :],
                       waits=[t_wi_cast_free[p],
                              tprev if (tt == 0 and fc == 0) else None,
                              tball if (tt == 0 and fc == 1) else None])
            tc = k.copyG(wibt[p][:], wst[p][:], waits=[td, t_wi_mm_done[p]])
            t_wi_cast_free[p] = tc
            pa = ps_a[:, p * 512:(p + 1) * 512]
            w0 = [tc, B["tXb"], t_inter_ev[fc] if tt > 0 else B.get("t_psa_free"),
                  t_inter_ev[(fc - 2) % 32] if (tt > 0 or fc >= 2) else None]
            tm = None
            for dcx in range(DC):
                tm = k.mm(pa, wibt[p][:, dcx * 128:(dcx + 1) * 128],
                          Xb[:, dcx * T + tt * 512: dcx * T + (tt + 1) * 512],
                          start=(dcx == 0), stop=False, waits=(w0 if dcx == 0 else []))
            tm = k.mm(pa, ballb[0:32, fc * 128:(fc + 1) * 128],
                      chT[0:32, tt * 512:(tt + 1) * 512], start=False, stop=True,
                      waits=[tball] + (tchT if (tt == 0 and fc == 0) else []))
            t_wi_mm_done[p] = tm
            tev = k.act(inter[:, fc * 512:(fc + 1) * 512], pa, AF.Relu,
                        bias=wib[:, fc:fc + 1], waits=[tm, twib])
            t_inter_ev[fc] = tev
        for oc in range(8):
            p = oc % 2
            tc = None
            for q2 in range(2):
                td = k.dma(wost[:], wo_h[oc, :, q2 * 2048:(q2 + 1) * 2048],
                           waits=[B.get("t_wost_free")])
                tc = k.copyG(wocb[p][:, q2 * 2048:(q2 + 1) * 2048], wost[:],
                             waits=[td, t_wocb_mm[p] if q2 == 0 else None])
                B["t_wost_free"] = tc
            slot = oc % 4
            pb = ps_b[:, slot * 512:(slot + 1) * 512]
            w0 = [tc, t_slot_ev[slot],
                  t_part1_done if (tt == 0 and oc == 0) else None,
                  tgbc if (tt == 0 and oc == 0) else None]
            w0 += [t_inter_ev[ic] for ic in range(32)] if oc == 0 else []
            tm = None
            for ic in range(32):
                tm = k.mm(pb, wocb[p][:, ic * 128:(ic + 1) * 128],
                          inter[:, ic * 512:(ic + 1) * 512],
                          start=(ic == 0), stop=(ic == 31), waits=(w0 if ic == 0 else []))
            t_wocb_mm[p] = tm
            tev = k.stt(X[:, oc * T + tt * 512: oc * T + (tt + 1) * 512], pb,
                        wob[:, oc:oc + 1], X[:, oc * T + tt * 512: oc * T + (tt + 1) * 512],
                        ALU.add, ALU.add, waits=[tm, twob, t_x_ev])
            t_slot_ev[slot] = tev
            t_x_ev = tev
    B["tX"] = t_x_ev
    B["t_psa_free"] = t_x_ev
    B["t_psb_av_free"] = t_x_ev
    B["tXb_free"] = Tok("T", len(k.ops["T"]) - 1)


def emit_ffn1(k, B, st_outer, xstack):
    nc = k.nc
    ps_a, ps_b = B["ps_a"], B["ps_b"]
    X, Xb = B["X"], B["Xb"]

    clsb = alloc(st_outer, nc, "f1clsb", [128, DC * 40], BF16)
    clsst = alloc(st_outer, nc, "f1clsst", [128, DC * 40], F32)
    ebst = alloc(st_outer, nc, "f1ebst", [1, 40], F32)
    ebbc = alloc(st_outer, nc, "f1ebbc", [128, 40], F32)
    lgc = alloc(st_outer, nc, "f1lgc", [128, 40], F32)
    nmx = alloc(st_outer, nc, "f1nmx", [128, 1], F32)
    es = alloc(st_outer, nc, "f1es", [128, 40], F32)
    ssum = alloc(st_outer, nc, "f1ssum", [128, 1], F32)
    rs = alloc(st_outer, nc, "f1rs", [128, 1], F32)
    scov = alloc(st_outer, nc, "f1sco", [128, 40], F32)
    sel = alloc(st_outer, nc, "f1sel", [128, 40], F32)
    m6 = alloc(st_outer, nc, "f1m6", [128, 1], F32)
    oh6 = alloc(st_outer, nc, "f1oh6", [128, 40], F32)
    wf = alloc(st_outer, nc, "f1wf", [128, 40], F32)
    zsc = alloc(st_outer, nc, "f1zsc", [128, 1], F32)
    zw1 = alloc(st_outer, nc, "f1zw1", [128, 16], F32)
    wr = alloc(st_outer, nc, "f1wr", [128, 16 * 32], F32)
    xts = [alloc(st_outer, nc, "f1xts0", [128, 1024], F32),
           alloc(st_outer, nc, "f1xts1", [128, 1024], F32)]

    cls_h = k.din("f1cls", k._W1["cls"])
    eb_h = k.din("f1eb", k._W1["eb"])
    wgu_h = k.din("f1wgu", k._W1["wgu"])
    wd_h = k.din("f1wd", k._W1["wd"])
    xtok = nc.dram_tensor("xtok", [T, 1024], F32, kind="Internal")
    y = nc.dram_tensor("y", [T, 1024], F32, kind="ExternalOutput")

    td = k.dma(clsst[:], cls_h[:])
    tcls = k.copyG(clsb[:], clsst[:], waits=[td])
    teb = k.dma(ebst[:], eb_h[:])
    tem = k.mm(ps_b[:, 1024:1064], B["onesrow_f"][0:1, :], ebst[0:1, :],
               start=True, stop=True, waits=[teb, B["t_consts"], B.get("t_psb_av_free")])
    tebc = k.copyV(ebbc[:], ps_b[:, 1024:1064], waits=[tem])

    t_xd = []
    t_rt_prev = None
    t_ps_tr_prev = B.get("t_psa_free")
    t_xts_free = [None, None]
    for s in range(16):
        w0 = [B["tXb"], tcls, t_rt_prev]
        tm = None
        for dcx in range(DC):
            tm = k.mm(ps_b[:, 0:40], Xb[:, dcx * T + s * 128: dcx * T + (s + 1) * 128],
                      clsb[:, dcx * 40:(dcx + 1) * 40], start=(dcx == 0), stop=(dcx == DC - 1),
                      waits=(w0 if dcx == 0 else []))
        tcp = k.copyV(lgc[:], ps_b[:, 0:40], waits=[tm])
        t_rt_prev = tcp
        tnm = k.red(nmx[:], lgc[:], ALU.max, negate=True, waits=[tcp])
        tex = k.act(es[:], lgc[:], AF.Exp, bias=nmx[:], accum=ssum[:], waits=[tnm])
        trs = k.recip(rs[:], ssum[:], waits=[tex])
        tsc = k.ts(scov[:], es[:], rs[:], None, ALU.mult, waits=[trs, tex])
        tsb = k.tt(lgc[:], scov[:], ebbc[:], ALU.add, waits=[tsc, tebc])
        tse = k.msetV(sel[:], 0.0, waits=[tsb])
        tprev = tse
        for r6 in range(6):
            tm6 = k.red(m6[:], lgc[:], ALU.max, waits=[tprev])
            th6 = k.ts(oh6[:], lgc[:], m6[:], None, ALU.is_equal, waits=[tm6])
            tsl = k.tt(sel[:], sel[:], oh6[:], ALU.max, waits=[th6])
            tprev = k.stt(lgc[:], oh6[:], -1e9, lgc[:], ALU.mult, ALU.add, waits=[th6, tsl])
        tw = k.stt(wf[:], scov[:], 2.5, sel[:], ALU.mult, ALU.mult, waits=[tprev, tsc])
        tzw = k.red(zsc[:], wf[:, 32:40], ALU.add, waits=[tw])
        k.ts(zw1[:, s:s + 1], zsc[:], 1.0, None, ALU.add, waits=[tzw])
        k.copyV(wr[:, s * 32:(s + 1) * 32], wf[:, 0:32], waits=[tw])
        ttr = None
        for dcx in range(DC):
            ttr = k.transp(ps_a[:, dcx * 128:(dcx + 1) * 128],
                           X[:, dcx * T + s * 128: dcx * T + (s + 1) * 128],
                           B["identf"][:],
                           waits=([B["tX"], B["t_ident"], t_ps_tr_prev] if dcx == 0 else []))
        txc = k.copyV(xts[s % 2][:], ps_a[:, 0:1024], waits=[ttr, t_xts_free[s % 2]])
        t_ps_tr_prev = txc
        txd = k.dma(xtok[s * 128:(s + 1) * 128, :], xts[s % 2][:], waits=[txc])
        t_xts_free[s % 2] = txd
        t_xd.append(txd)
    t_route_done = Tok("V", len(k.ops["V"]) - 1)

    k.barrier()
    xstack.close()

    stb = ExitStack()
    st_outer.enter_context(stb)
    oacc = alloc(stb, nc, "f1oacc", [128, 16 * 1024], F32)
    wgub = [alloc(stb, nc, "f1wgub0", [128, DC * 1024], BF16),
            alloc(stb, nc, "f1wgub1", [128, DC * 1024], BF16)]
    wdb = [alloc(stb, nc, "f1wdb0", [128, 4 * 1024], BF16),
           alloc(stb, nc, "f1wdb1", [128, 4 * 1024], BF16)]
    wstg = [alloc(stb, nc, "f1wstg0", [128, 1024], F32),
            alloc(stb, nc, "f1wstg1", [128, 1024], F32)]
    sg = alloc(stb, nc, "f1sg", [128, 512], BF16)
    iw = alloc(stb, nc, "f1iw", [128, 512], F32)
    intT = alloc(stb, nc, "f1intT", [128, 512], BF16)
    xf = alloc(stb, nc, "f1xf", [128, 1024], F32)
    xc = alloc(stb, nc, "f1xc", [128, 1024], F32)
    s1 = alloc(stb, nc, "f1s1", [128, 1], F32)
    mn = alloc(stb, nc, "f1mn", [128, 1], F32)
    v2 = alloc(stb, nc, "f1v2", [128, 1], F32)
    vv = alloc(stb, nc, "f1vv", [128, 1], F32)
    osb = [alloc(stb, nc, "f1osb0", [128, 1024], F32),
           alloc(stb, nc, "f1osb1", [128, 1024], F32)]

    t_wcast_done = [None, None]
    t_wfree = [None, None]
    t_stg_free = [None, None]
    t_ev = {}
    t_sg_free = None
    t_iw_free = None
    t_int_free = None
    t_gu_slot_free = [None, None]
    t_tr_free = None
    t_y_free = None
    for e in range(32):
        p = e % 2
        tcasts = []
        for hc in range(DC):
            sp = hc % 2
            td = k.dma(wstg[sp][:], wgu_h[e, :, hc * 1024:(hc + 1) * 1024],
                       waits=[t_stg_free[sp]])
            tc = k.copyG(wgub[p][:, hc * 1024:(hc + 1) * 1024], wstg[sp][:],
                         waits=[td, t_wfree[p] if hc == 0 else None])
            t_stg_free[sp] = tc
            tcasts.append(tc)
        for ic in range(4):
            sp = ic % 2
            td = k.dma(wstg[sp][:], wd_h[e, :, ic * 1024:(ic + 1) * 1024],
                       waits=[t_stg_free[sp]])
            tc = k.copyG(wdb[p][:, ic * 1024:(ic + 1) * 1024], wstg[sp][:],
                         waits=[td, t_wfree[p] if ic == 0 else None])
            t_stg_free[sp] = tc
            tcasts.append(tc)
        t_wcast_done[p] = tcasts
        t_last_mm = None
        for s in range(16):
            gsl = s % 2
            pg = ps_a[:, gsl * 1024:(gsl + 1) * 1024]
            w0 = tcasts + [B["tXb"], t_gu_slot_free[gsl]]
            tm = None
            for hc in range(DC):
                for j in range(2):
                    tm = k.mm(pg[:, j * 512:(j + 1) * 512],
                              Xb[:, hc * T + s * 128: hc * T + (s + 1) * 128],
                              wgub[p][:, hc * 1024 + j * 512: hc * 1024 + (j + 1) * 512],
                              start=(hc == 0), stop=(hc == DC - 1),
                              waits=(w0 if (hc == 0 and j == 0) else []))
            tsg = k.act(sg[:], pg[:, 0:512], AF.Silu, waits=[tm, t_sg_free])
            tiw = k.stt(iw[:], sg[:], wr[:, s * 32 + e: s * 32 + e + 1], pg[:, 512:1024],
                        ALU.mult, ALU.mult, waits=[tsg, t_route_done, t_iw_free])
            t_gu_slot_free[gsl] = tiw
            t_sg_free = tiw
            ttr = None
            for j in range(4):
                ttr = k.transp(ps_b[:, 1024 + j * 128: 1024 + (j + 1) * 128],
                               iw[:, j * 128:(j + 1) * 128], B["identf"][:],
                               waits=([tiw, t_tr_free if t_tr_free is not None else tebc]
                                      if j == 0 else []))
            t_iw_free = ttr
            tic = k.copyA(intT[:], ps_b[:, 1024:1536], waits=[ttr, t_int_free])
            t_tr_free = tic
            tdn = None
            w1 = [tic, t_y_free, t_ev.get((e - 1, s))]
            for ic in range(4):
                for j in range(2):
                    tdn = k.mm(ps_b[:, j * 512:(j + 1) * 512],
                               intT[:, ic * 128:(ic + 1) * 128],
                               wdb[p][:, ic * 1024 + j * 512: ic * 1024 + (j + 1) * 512],
                               start=(ic == 0), stop=(ic == 3),
                               waits=(w1 if (ic == 0 and j == 0) else []))
            t_int_free = tdn
            t_last_mm = tdn
            if e == 0:
                tev = k.copyV(oacc[:, s * 1024:(s + 1) * 1024], ps_b[:, 0:1024], waits=[tdn])
            else:
                tev = k.tt(oacc[:, s * 1024:(s + 1) * 1024], ps_b[:, 0:1024],
                           oacc[:, s * 1024:(s + 1) * 1024], ALU.add,
                           waits=[tdn, t_ev[(e - 1, s)]])
            t_ev[(e, s)] = tev
            t_y_free = tev
        t_wfree[p] = t_last_mm

    t_out_dma = []
    t_xts2_free = [t_xd[-1], t_xd[-1]]
    t_osb_free = [None, None]
    t_prev_sq = None
    for s in range(16):
        tld = k.dma(xts[s % 2][:], xtok[s * 128:(s + 1) * 128, :],
                    waits=[t_xts2_free[s % 2], t_xd[s]])
        tf = k.stt(xf[:], xts[s % 2][:], zw1[:, s:s + 1], oacc[:, s * 1024:(s + 1) * 1024],
                   ALU.mult, ALU.add, waits=[tld, t_ev[(31, s)], t_prev_sq])
        t_xts2_free[s % 2] = tf
        ts1 = k.red(s1[:], xf[:], ALU.add, waits=[tf])
        tmn = k.ts(mn[:], s1[:], 1.0 / 1024.0, None, ALU.mult, waits=[ts1])
        txc = k.ts(xc[:], xf[:], mn[:], None, ALU.subtract, waits=[tmn, t_prev_sq])
        tsq = k.act(xf[:], xc[:], AF.Square, accum=v2[:], waits=[txc])
        t_prev_sq = tsq
        tvv = k.ts(vv[:], v2[:], 1.0 / 1024.0, None, ALU.mult, waits=[tsq])
        tl = k.act(vv[:], vv[:], AF.Ln, bias=B["epsc"][:], waits=[tvv])
        tr2 = k.act(vv[:], vv[:], AF.Exp, scale=-0.5, waits=[tl])
        to = k.ts(osb[s % 2][:], xc[:], vv[:], None, ALU.mult,
                  waits=[tr2, txc, t_osb_free[s % 2]])
        tod = k.dma(y[s * 128:(s + 1) * 128, :], osb[s % 2][:], waits=[to])
        t_osb_free[s % 2] = tod
        t_out_dma.append(tod)
    k.wait_only("S", [t_out_dma[-1]])



def build_program(A, blocks):
    nc = bass.Bass("TRN2", target_bir_lowering=False)
    k = KB(nc)
    k._W1 = A["f1"]
    B = {}
    with ExitStack() as outer:
        B["ps_a"] = outer.enter_context(nc.psum_tensor("ps_a", [128, 2048], F32))
        B["ps_b"] = outer.enter_context(nc.psum_tensor("ps_b", [128, 2048], F32))
        sems = {e: outer.enter_context(nc.semaphore("sem" + e)) for e in KB.ENGS}
        emit_consts(k, B, outer)

        B["Xb"] = alloc(outer, nc, "Xb", [128, DC * T], BF16)

        xstack = ExitStack()
        outer.enter_context(xstack)
        B["X"] = alloc(xstack, nc, "X", [128, DC * T], F32, side="right")

        emit_load_x(k, B, A["x"])

        for bi, bl in enumerate(blocks):
            if bi > 0:
                k.barrier()
            if bl == "attn0" or bl == "attn1":
                W = A["a0"] if bl == "attn0" else A["a1"]
                with ExitStack() as ph:
                    lnscr = (alloc(ph, nc, bl + "lnm", [128, 1024], F32),
                             alloc(ph, nc, bl + "lnv", [128, 1024], F32),
                             alloc(ph, nc, bl + "lnq", [128, 1024], BF16))
                    emit_attn(k, B, ph, W, has_bias=(bl == "attn0"), pre=bl[-1] + "_")
                    emit_ln(k, B, lnscr)
                    k.barrier()
            elif bl == "ffn0":
                with ExitStack() as ph:
                    lnscr = (alloc(ph, nc, bl + "lnm", [128, 1024], F32),
                             alloc(ph, nc, bl + "lnv", [128, 1024], F32),
                             alloc(ph, nc, bl + "lnq", [128, 1024], BF16))
                    emit_ffn0(k, B, ph, A["f0"])
                    emit_ln(k, B, lnscr)
                    k.barrier()
            elif bl == "ffn1":
                emit_ffn1(k, B, outer, xstack)
        if blocks[-1] != "ffn1":
            emit_store_x(k, B)
        k.finalize(sems)
    return nc, k



def _run(nc, k, xarr_percore, n_cores=NCORES):
    in_maps = []
    for c in range(n_cores):
        m = {}
        for name, (kind, arr) in k.host.items():
            if kind == "percore":
                m[name] = np.ascontiguousarray(arr[c])
            else:
                m[name] = arr
        in_maps.append(m)
    res = run_bass_kernel_spmd(nc, in_maps, core_ids=list(range(n_cores)))
    return res.results


def kernel(**inputs):
    A = _arrange(inputs)
    if FUSED:
        nc, k = build_program(A, ["attn0", "ffn0", "attn1", "ffn1"])
        res = _run(nc, k, A["x"])
        return np.stack([r["y"] for r in res], axis=0)
    else:
        xcur = A["x"]
        for blocks in (["attn0"], ["ffn0"], ["attn1"]):
            nc, k = build_program(A, blocks)
            k.host["x"] = ("percore", xcur)
            res = _run(nc, k, xcur)
            xcur = np.stack([r["xout"] for r in res], axis=0)
        nc, k = build_program(A, ["ffn1"])
        k.host["x"] = ("percore", xcur)
        res = _run(nc, k, xcur)
        return np.stack([r["y"] for r in res], axis=0)


if __name__ == "__main__":
    rng = np.random.default_rng(0)
    fake = {}
    import json
    spec = json.load(open("/root/problem/spec.json"))
    for s in spec["input_specs"]:
        fake[s["name"]] = rng.standard_normal(s["shape"]).astype(np.float32)
    A = _arrange(fake)
    for blocks in (["attn0"], ["ffn0"], ["attn1"], ["ffn1"],
                   ["attn0", "ffn0", "attn1", "ffn1"]):
        nc, k = build_program(A, blocks)
        print(blocks, k.n_ops())


# version 05
# speedup vs baseline: 1.2403x; 1.2403x over previous
import math
import numpy as np
from contextlib import ExitStack

import concourse.bass as bass
import concourse.mybir as mybir
from concourse.bass_utils import run_bass_kernel_spmd

F32 = mybir.dt.float32
BF16 = mybir.dt.bfloat16
AF = mybir.ActivationFunctionType
ALU = mybir.AluOpType
AX = mybir.AxisListType

NCORES = 8
T = 2048
D = 1024
DC = 8
NH = 16
HD = 64
HPG = 4
NGR = NH // HPG
SC = float(1.0 / math.sqrt(8.0))
EPS = 1e-5

FUSED = True
DEBUG_FFN0 = False



class Tok:
    __slots__ = ("eng", "idx")

    def __init__(self, eng, idx):
        self.eng = eng
        self.idx = idx


class KB:
    ENGS = ("T", "A", "V", "G", "S")

    def __init__(self, nc):
        self.nc = nc
        self.ops = {e: [] for e in self.ENGS}
        self.host = {}
        self.gdr = None

    def op(self, eng, fn, waits=(), dma=False):
        wl = []
        for w in waits:
            if w is None:
                continue
            self.ops[w.eng][w.idx][3] = True
            wl.append(w)
        self.ops[eng].append([fn, wl, dma, dma, None])
        return Tok(eng, len(self.ops[eng]) - 1)

    def mm(self, out, lhsT, rhs, start, stop, waits=()):
        return self.op("T", lambda e: e.matmul(out, lhsT, rhs, start=start, stop=stop), waits)

    def transp(self, out, in_, ident, waits=()):
        return self.op("T", lambda e: e.transpose(out, in_, ident), waits)

    def act(self, out, in_, func, bias=0.0, scale=1.0, accum=None, waits=()):
        return self.op("A", lambda e: e.activation(out, in_, func, bias=bias, scale=scale, accum_out=accum), waits)

    def copyA(self, out, in_, waits=()):
        return self.op("A", lambda e: e.activation(out, in_, AF.Identity), waits)

    def tt(self, out, a, b, aluop, waits=()):
        return self.op("V", lambda e: e.tensor_tensor(out, a, b, aluop), waits)

    def ts(self, out, in0, s1, s2, op0, op1=ALU.bypass, waits=()):
        if s2 is None:
            return self.op("V", lambda e: e.tensor_scalar(out, in0, s1, None, op0), waits)
        return self.op("V", lambda e: e.tensor_scalar(out, in0, s1, s2, op0, op1), waits)

    def stt(self, out, in0, s, in1, op0, op1, waits=()):
        return self.op("V", lambda e: e.scalar_tensor_tensor(out, in0, s, in1, op0, op1), waits)

    def copyV(self, out, in_, waits=()):
        return self.op("V", lambda e: e.tensor_copy(out, in_), waits)

    def copyG(self, out, in_, waits=()):
        t = self.op("G", lambda e: e.tensor_copy(out, in_), waits)
        return self._gdrain(t)

    def _gdrain(self, t):
        return self.op("G", lambda e: e.memset(self.gdr[:, 0:1], 0.0), waits=[t])

    def red(self, out, in_, aluop, negate=False, waits=()):
        return self.op("V", lambda e: e.tensor_reduce(out, in_, AX.X, aluop, negate=negate), waits)

    def recip(self, out, in_, waits=()):
        return self.op("V", lambda e: e.reciprocal(out, in_), waits)

    def msetV(self, ap, val, waits=()):
        return self.op("V", lambda e: e.memset(ap, val), waits)

    def msetG(self, ap, val, waits=()):
        t = self.op("G", lambda e: e.memset(ap, val), waits)
        return self._gdrain(t)

    def dma(self, out, in_, waits=()):
        return self.op("S", lambda e: e.dma_start(out, in_), waits, dma=True)

    def barrier(self):
        toks = [Tok(e, len(self.ops[e]) - 1) for e in self.ENGS if self.ops[e]]
        for t in toks:
            self.ops[t.eng][t.idx][3] = True
        for e in self.ENGS:
            self.op(e, lambda eng: eng.nop(), waits=toks)

    def wait_only(self, eng, waits):
        return self.op(eng, lambda e: e.nop(), waits=waits)

    def drainV(self, dscr, tok):
        return self.op("V", lambda e: e.memset(dscr[0:1, 0:1], 0.0), waits=[tok])

    def drainA(self, dscr, tok):
        return self.op("A", lambda e: e.activation(dscr[0:1, 1:2], dscr[0:1, 0:1], AF.Identity),
                       waits=[tok])

    def din(self, name, arr, percore=False):
        h = self.nc.dram_tensor(name, list(arr.shape), F32, kind="ExternalInput")
        self.host[name] = ("percore" if percore else "shared", arr)
        return h

    def finalize(self, sems):
        for e in self.ENGS:
            c = 0
            for rec in self.ops[e]:
                if rec[3]:
                    c += 16 if rec[2] else 1
                    rec[4] = c

        engmap = {"T": "tensor", "A": "scalar", "V": "vector", "G": "gpsimd", "S": "sync"}
        with self.nc.Block() as block:
            for e in self.ENGS:
                if not self.ops[e]:
                    continue

                def mk(eng_key):
                    def f(eng):
                        for rec in self.ops[eng_key]:
                            wa = {}
                            for t in rec[1]:
                                v = self.ops[t.eng][t.idx][4]
                                assert v is not None
                                if wa.get(t.eng, 0) < v:
                                    wa[t.eng] = v
                            for s, v in sorted(wa.items()):
                                eng.wait_ge(sems[s], v)
                            if rec[0] is not None:
                                ins = rec[0](eng)
                                if rec[3]:
                                    ins.then_inc(sems[eng_key], 16 if rec[2] else 1)
                            else:
                                assert not rec[3]
                    return f

                getattr(block, engmap[e])(mk(e))

    def n_ops(self):
        return {e: len(self.ops[e]) for e in self.ENGS}



def _c(a):
    return np.ascontiguousarray(a, dtype=np.float32)


def _arrange(inp):
    A = {}
    x = np.asarray(inp["L0_attn_data"], dtype=np.float32)
    A["x"] = _c(x.transpose(0, 2, 1).reshape(NCORES, DC, 128, T)
                .transpose(0, 2, 1, 3)).reshape(NCORES, 128, DC * T)

    def arr_attn(Wqkv, b_qkv, b_out):
        d = {}
        wqk = Wqkv[:, :2048].reshape(DC, 128, 2, 8, 128)
        d["wqk"] = _c(wqk.transpose(2, 3, 1, 0, 4)).reshape(2, 8, 128, DC * 128)
        wv = Wqkv[:, 2048:].reshape(DC, 128, NGR, HPG * HD)
        d["wv"] = _c(wv.transpose(2, 1, 0, 3)).reshape(NGR, 128, DC * HPG * HD)
        if b_qkv is not None:
            d["bqkv"] = _c(np.asarray(b_qkv)[:2048].reshape(16, 2, 64)
                           .transpose(1, 2, 0)).reshape(64, 32)
            d["bv"] = _c(np.asarray(b_qkv)[2048:].reshape(1, 1024))
            d["bout"] = _c(np.asarray(b_out).reshape(8, 128).T)
        return d

    l0w = np.asarray(inp["L0_attn_W_qkv"], dtype=np.float32)
    a0 = arr_attn(l0w, inp["L0_attn_b_qkv"], inp["L0_attn_b_out"])
    a0["wout"] = _c(np.asarray(inp["L0_attn_W_out"], dtype=np.float32).reshape(16, 64, 1024))
    A["a0"] = a0

    wq = np.asarray(inp["L1_attn_W_q"], dtype=np.float32).transpose(1, 0, 2).reshape(D, D)
    wk = np.asarray(inp["L1_attn_W_k"], dtype=np.float32).transpose(1, 0, 2).reshape(D, D)
    wv1 = np.asarray(inp["L1_attn_W_v"], dtype=np.float32).transpose(1, 0, 2).reshape(D, D)
    l1w = np.concatenate([wq, wk, wv1], axis=1)
    a1 = arr_attn(l1w, None, None)
    a1["wout"] = _c(np.asarray(inp["L1_attn_W_o"], dtype=np.float32).reshape(16, 64, 1024))
    A["a1"] = a1

    f0 = {}
    gW = np.asarray(inp["L0_ffn_gate_W"], dtype=np.float32)
    lA = np.asarray(inp["L0_ffn_lora_A"], dtype=np.float32)
    ga = np.concatenate([gW.T.reshape(DC, 128, 8),
                         _c(lA.reshape(32, D).T).reshape(DC, 128, 32)], axis=2)
    f0["ga"] = _c(ga.transpose(1, 0, 2)).reshape(128, DC * 40)
    f0["gb"] = _c(np.asarray(inp["L0_ffn_gate_b"]).reshape(1, 8))
    f0["ball"] = _c(np.asarray(inp["L0_ffn_lora_B"], dtype=np.float32).transpose(0, 2, 1).reshape(32, 4096))
    wi = np.asarray(inp["L0_ffn_wi_W"], dtype=np.float32)
    f0["wi"] = _c(_c(wi.T).reshape(DC, 128, 32, 128).transpose(2, 1, 0, 3)).reshape(32, 128, DC * 128)
    f0["wib"] = _c(np.asarray(inp["L0_ffn_wi_b"]).reshape(32, 128).T)
    wo = np.asarray(inp["L0_ffn_wo_W"], dtype=np.float32)
    f0["wo"] = _c(wo.reshape(8, 128, 32, 128).transpose(0, 3, 2, 1)).reshape(8, 128, 32 * 128)
    f0["wob"] = _c(np.asarray(inp["L0_ffn_wo_b"]).reshape(8, 128).T)
    A["f0"] = f0

    f1 = {}
    cls = np.asarray(inp["L1_ffn_classifier_weight"], dtype=np.float32)
    f1["cls"] = _c(_c(cls.T).reshape(DC, 128, 40).transpose(1, 0, 2)).reshape(128, DC * 40)
    f1["eb"] = _c(np.asarray(inp["L1_ffn_e_score_correction_bias"]).reshape(1, 40))
    gu = np.asarray(inp["L1_ffn_gate_up_proj"], dtype=np.float32)[:32]
    f1["wgu"] = _c(gu.transpose(0, 2, 1).reshape(32, DC, 128, 1024)
                   .transpose(0, 2, 1, 3)).reshape(32, 128, DC * 1024)
    dn = np.asarray(inp["L1_ffn_down_proj"], dtype=np.float32)
    f1["wd"] = _c(dn.transpose(0, 2, 1).reshape(32, 4, 128, 1024)
                  .transpose(0, 2, 1, 3)).reshape(32, 128, 4 * 1024)
    A["f1"] = f1
    return A



def alloc(st, nc, name, shape, dtype, side=None):
    if side is None:
        return st.enter_context(nc.sbuf_tensor(name, shape, dtype))
    return st.enter_context(nc.sbuf_tensor(name, shape, dtype, side=side))


def emit_consts(k, B, st):
    nc = k.nc
    B["identf"] = alloc(st, nc, "identf", [128, 128], F32)
    B["iscr"] = alloc(st, nc, "iscr", [128, 128], F32)
    B["cm"] = alloc(st, nc, "cm", [128, 128], BF16)
    B["ones_bf"] = alloc(st, nc, "ones_bf", [128, 1], BF16)
    B["onesrow_bf"] = alloc(st, nc, "onesrow_bf", [1, 128], BF16)
    B["onesrow_f"] = alloc(st, nc, "onesrow_f", [1, 128], F32)
    B["epsc"] = alloc(st, nc, "epsc", [128, 1], F32)
    B["ones64r"] = alloc(st, nc, "ones64r", [65, 64], F32)
    B["dscr"] = alloc(st, nc, "dscr", [128, 4], F32)
    t1 = k.op("G", lambda e: e.iota(B["iscr"][:], [[1, 128]], channel_multiplier=-1,
                                    allow_small_or_imprecise_dtypes=True))
    t2 = k.op("G", lambda e: e.tensor_scalar(B["identf"][:], B["iscr"][:], 0.0, None, ALU.is_equal),
              waits=[t1])
    B["t_ident"] = k._gdrain(t2)
    k.msetG(B["cm"][:], 1.0 / 1024.0)
    k.msetG(B["ones_bf"][:], 1.0)
    k.msetG(B["onesrow_bf"][:], 1.0)
    k.msetG(B["onesrow_f"][:], 1.0)
    k.msetG(B["ones64r"][:], 1.0)
    B["t_consts"] = k.msetG(B["epsc"][:], EPS)


def emit_load_x(k, B, xarr):
    xin = k.din("x", xarr[0], percore=True)
    k.host["x"] = ("percore", xarr)
    tl = k.dma(B["X"][:], xin[:])
    B["tX"] = tl
    tc = k.copyV(B["Xb"][:], B["X"][:], waits=[tl])
    B["tXb"] = k.drainV(B["dscr"], tc)
    B["tXb_free"] = None


def emit_store_x(k, B):
    nc = k.nc
    xout = nc.dram_tensor("xout", [128, DC * T], F32, kind="ExternalOutput")
    tdr = k.drainV(B["dscr"], B["tX"])
    td = k.dma(xout[:], B["X"][:], waits=[B["tX"], B["tXb"], tdr])
    k.wait_only("S", [td])


def emit_attn(k, B, st, W, has_bias, pre):
    nc = k.nc
    ps_a, ps_b = B["ps_a"], B["ps_b"]
    Xb, X = B["Xb"], B["X"]

    qa = alloc(st, nc, pre + "qa", [65, T], BF16)
    qb = alloc(st, nc, pre + "qb", [65, T], BF16)
    ka = alloc(st, nc, pre + "ka", [65, T], BF16)
    kb = alloc(st, nc, pre + "kb", [65, T], BF16)
    vt = alloc(st, nc, pre + "vt", [128, 16 * HPG * 65], BF16)
    av = alloc(st, nc, pre + "av", [64, HPG * T], BF16)
    pt = [alloc(st, nc, pre + "pt0", [128, T], BF16),
          alloc(st, nc, pre + "pt1", [128, T], BF16)]
    wst = alloc(st, nc, pre + "wst", [128, 1024], F32)
    wqb = alloc(st, nc, pre + "wqb", [128, 1024], BF16)
    wkb = alloc(st, nc, pre + "wkb", [128, 1024], BF16)
    wvb = alloc(st, nc, pre + "wvb", [128, DC * HPG * HD], BF16)
    wob = alloc(st, nc, pre + "wob", [64, HPG * 1024], BF16)
    sq = alloc(st, nc, pre + "sq", [64, T], BF16)
    scf = alloc(st, nc, pre + "scf", [65, T], F32)
    scr = alloc(st, nc, pre + "scr", [1, 8], F32)

    wqk_h = k.din(pre + "wqk", W["wqk"])
    wv_h = k.din(pre + "wv", W["wv"])
    wout_h = k.din(pre + "wout", W["wout"])
    if has_bias:
        bqs = alloc(st, nc, pre + "bqs", [64, 32], F32)
        bvst = alloc(st, nc, pre + "bvst", [1, 1024], F32)
        bvb = alloc(st, nc, pre + "bvb", [1, 1024], BF16)
        bob = alloc(st, nc, pre + "bob", [128, 8], F32)
        tb = k.dma(bqs[:], k.din(pre + "bqkv", W["bqkv"])[:])
        t_bqs0 = k.ts(bqs[:], bqs[:], SC, None, ALU.mult, waits=[tb])
        t_bqs = k.drainV(B["dscr"], t_bqs0)
        tb2 = k.dma(bvst[:], k.din(pre + "bv", W["bv"])[:])
        t_bvb = k.copyG(bvb[:], bvst[:], waits=[tb2])
        t_bob = k.dma(bob[:], k.din(pre + "bout", W["bout"])[:])
    else:
        t_bqs = t_bvb = t_bob = None

    t_ka1 = k.msetG(ka[64:65, :], 1.0)
    t_kb1 = k.msetG(kb[64:65, :], 1.0)

    tXb = B["tXb"]
    t_prev_opmm = [None] * 8
    t_evac_x = B["tX"]

    for g in range(NGR):
        tw = []
        for q4 in range(2):
            sl = slice(q4 * 1024, (q4 + 1) * 1024)
            td = k.dma(wst[:, :], wv_h[g, :, sl], waits=[B.get("t_wst_free")])
            tc = k.copyG(wvb[:, sl], wst[:, :],
                         waits=[td, B.get("t_wvb_free") if q4 == 0 else None])
            B["t_wst_free"] = tc
            tw.append(tc)
        t_vp_evac = [None] * 4
        t_vt_set = k.msetG(vt[:], 1.0, waits=[B.get("t_vt_free")])
        t_vt = []
        for tt in range(16):
            slot = tt % 4
            pb = ps_b[:, slot * 512: slot * 512 + 256]
            mmw = [tXb] + tw + [t_vp_evac[slot], B.get("t_psb_free")]
            tm = None
            for dcx in range(DC):
                tm = k.mm(pb, Xb[:, dcx * T + tt * 128: dcx * T + (tt + 1) * 128],
                          wvb[:, dcx * 256:(dcx + 1) * 256],
                          start=(dcx == 0), stop=(not has_bias and dcx == DC - 1),
                          waits=(mmw if dcx == 0 else []))
            if has_bias:
                tm = k.mm(pb, B["onesrow_bf"][0:1, 0:128],
                          bvb[0:1, g * 256:(g + 1) * 256], start=False, stop=True,
                          waits=[t_bvb] if tt == 0 else [])
            dst = vt[:, tt * (HPG * 65):(tt + 1) * (HPG * 65)] \
                .rearrange("p (h e) -> p h e", h=HPG)[:, :, 0:HD]
            src = pb.rearrange("p (h e) -> p h e", h=HPG)
            te = k.copyV(dst, src, waits=[tm, t_vt_set])
            t_vp_evac[slot] = te
            t_vt.append(te)
        t_vt_last = k.drainV(B["dscr"], t_vt[-1])
        B["t_psb_free"] = t_vt_last

        for pr in range(HPG // 2):
            c = g * (HPG // 2) + pr
            tdq = k.dma(wst[:, :], wqk_h[0, c, :, :], waits=[B.get("t_wst_free")])
            tcq = k.copyG(wqb[:, :], wst[:, :], waits=[tdq, B.get("t_wqb_free")])
            tdk = k.dma(wst[:, :], wqk_h[1, c, :, :], waits=[tcq])
            tck = k.copyG(wkb[:, :], wst[:, :], waits=[tdk, B.get("t_wkb_free")])
            B["t_wst_free"] = tck

            t_ev = []
            for s, wb, tiles in ((0, wqb, (qa, qb)), (1, wkb, (ka, kb))):
                tcast = tcq if s == 0 else tck
                for tt in range(4):
                    for hl in (0, 1):
                        pa = ps_a[0:64, (tt % 2) * 1024 + hl * 512:
                                  (tt % 2) * 1024 + (hl + 1) * 512]
                        w0 = [tXb, tcast, B.get("t_psa_q%d%d" % (tt % 2, hl))]
                        tm = None
                        for dcx in range(DC):
                            tm = k.mm(pa,
                                      wb[:, dcx * 128 + hl * 64: dcx * 128 + hl * 64 + 64],
                                      Xb[:, dcx * T + tt * 512: dcx * T + (tt + 1) * 512],
                                      start=(dcx == 0), stop=(dcx == DC - 1),
                                      waits=(w0 if dcx == 0 else []))
                        if has_bias:
                            bias = bqs[0:64, (s * 8 + c) + hl * 16: (s * 8 + c) + hl * 16 + 1]
                        else:
                            bias = 0.0
                        ev = k.act(tiles[hl][0:64, tt * 512:(tt + 1) * 512], pa,
                                   AF.Identity, bias=bias, scale=SC,
                                   waits=[tm] + ([t_bqs] if has_bias else []))
                        t_ev.append(ev)
                        B["t_psa_q%d%d" % (tt % 2, hl)] = ev
            t_proj = t_ev[-1]

            t_qrow = {}
            for hl, (qt, kt) in enumerate(((qa, ka), (qb, kb))):
                tsq0 = k.tt(sq[:, :], kt[0:64, :], kt[0:64, :], ALU.mult,
                            waits=[t_proj, B.get("t_sq_free")])
                tsq = k.drainV(B["dscr"], tsq0)
                tr = []
                for j in range(4):
                    tr.append(k.mm(ps_b[0:1, j * 512:(j + 1) * 512],
                                   B["ones_bf"][0:64, 0:1], sq[:, j * 512:(j + 1) * 512],
                                   start=True, stop=True,
                                   waits=[tsq, B.get("t_psbrow_free"),
                                          B.get("t_psb_free"), B["t_consts"]]))
                tkmx = k.red(scr[0:1, 0:1], ps_b[0:1, 0:2048], ALU.max, waits=tr)
                tsq2a = k.tt(sq[:, :], qt[0:64, :], qt[0:64, :], ALU.mult,
                             waits=[t_proj] + tr)
                tsq2 = k.drainV(B["dscr"], tsq2a)
                tq = []
                for j in range(4):
                    tq.append(k.mm(ps_b[0:1, j * 512:(j + 1) * 512],
                                   B["ones_bf"][0:64, 0:1], sq[:, j * 512:(j + 1) * 512],
                                   start=True, stop=True, waits=[tsq2, tkmx]))
                trow = k.ts(sq[0:1, :], ps_b[0:1, 0:2048], scr[0:1, 0:1], -0.5,
                            ALU.add, ALU.mult, waits=tq + [tkmx])
                tdr = k.drainV(B["dscr"], trow)
                tmv = k.dma(qt[64:65, :], sq[0:1, :], waits=[trow, tdr])
                B["t_sq_free"] = tmv
                t_qrow[hl] = tmv
                B["t_psbrow_free"] = trow

            for hl, (qt, kt) in enumerate(((qa, ka), (qb, kb))):
                hh = pr * 2 + hl
                t_exp = {}
                t_avmm = {}
                for kc in range(16):
                    for qh in (0, 1):
                        if kc == 0:
                            stw = [t_qrow[hl], B.get("t_psa_free"), t_ka1, t_kb1]
                        else:
                            stw = [t_exp[(kc - 1, qh)]]
                        tm = None
                        for jj in (0, 1):
                            col = qh * 1024 + jj * 512
                            tm = k.mm(ps_a[:, col:col + 512],
                                      kt[:, kc * 128:(kc + 1) * 128],
                                      qt[:, col:col + 512],
                                      start=True, stop=True,
                                      waits=(stw if jj == 0 else []))
                        t_exp[(kc, qh, "st")] = tm
                    for qh in (0, 1):
                        ew = [t_exp[(kc, qh, "st")]]
                        if kc >= 2:
                            ew.append(t_avmm[(kc - 2, qh)])
                        else:
                            ew.append(B.get("t_pt_free%d%d" % (kc % 2, qh)))
                        te = k.act(pt[kc % 2][:, qh * 1024:(qh + 1) * 1024],
                                   ps_a[:, qh * 1024:(qh + 1) * 1024], AF.Exp, waits=ew)
                        t_exp[(kc, qh)] = te
                    src_kc = kc - 1
                    if src_kc >= 0:
                        for qh in (0, 1):
                            avw = [t_exp[(src_kc, qh)]]
                            if src_kc == 0:
                                avw += [B.get("t_psb_free"), B.get("t_psbrow_free"),
                                        t_vt_last, t_qrow[1]]
                            ta = None
                            for jj in (0, 1):
                                col = qh * 1024 + jj * 512
                                ta = k.mm(ps_b[0:65, col:col + 512],
                                          vt[:, (src_kc * HPG + hh) * 65:
                                             (src_kc * HPG + hh) * 65 + 65],
                                          pt[src_kc % 2][:, col:col + 512],
                                          start=(src_kc == 0), stop=(src_kc == 15),
                                          waits=(avw if jj == 0 else []))
                            t_avmm[(src_kc, qh)] = ta
                for qh in (0, 1):
                    ta = None
                    for jj in (0, 1):
                        col = qh * 1024 + jj * 512
                        ta = k.mm(ps_b[0:65, col:col + 512],
                                  vt[:, (15 * HPG + hh) * 65:(15 * HPG + hh) * 65 + 65],
                                  pt[15 % 2][:, col:col + 512],
                                  start=False, stop=True,
                                  waits=([t_exp[(15, qh)]] if jj == 0 else []))
                    t_avmm[(15, qh)] = ta
                B["t_pt_free00"] = t_avmm[(14, 0)]
                B["t_pt_free01"] = t_avmm[(14, 1)]
                B["t_pt_free10"] = t_avmm[(15, 0)]
                B["t_pt_free11"] = t_avmm[(15, 1)]
                t_av_last = [t_avmm[(15, 0)], t_avmm[(15, 1)]]

                trc = k.recip(scf[64:65, :], ps_b[64:65, 0:2048], waits=t_av_last)
                tdrc = k.drainV(B["dscr"], trc)
                tlb = []
                for j in range(4):
                    tlb.append(k.mm(ps_a[0:64, j * 512:(j + 1) * 512],
                                    B["ones64r"][64:65, 0:64],
                                    scf[64:65, j * 512:(j + 1) * 512],
                                    start=True, stop=True,
                                    waits=[trc, tdrc,
                                           t_exp[(15, 0)] if j == 0 else None,
                                           t_exp[(15, 1)] if j == 0 else None,
                                           B["t_consts"] if j == 0 else None]))
                tlbc = k.copyV(scf[0:64, :], ps_a[0:64, 0:2048], waits=tlb)
                tev = k.stt(av[0:64, hh * T:(hh + 1) * T],
                            ps_b[0:64, 0:2048], 1.0, scf[0:64, :], ALU.mult, ALU.mult,
                            waits=[tlbc, B.get("t_av_free")] + t_av_last)
                B["t_psa_free"] = tlbc
                B["t_psb_free"] = tev
                B["t_av_evac%d" % hh] = tev
            for key in ("t_psa_q00", "t_psa_q01", "t_psa_q10", "t_psa_q11"):
                B[key] = B["t_psa_free"]

        tow = []
        for hh in range(HPG):
            h_glob = g * HPG + hh
            td = k.dma(wst[0:64, :], wout_h[h_glob, :, :], waits=[B.get("t_wst_free")])
            tc = k.copyG(wob[0:64, hh * 1024:(hh + 1) * 1024], wst[0:64, :],
                         waits=[td, B.get("t_wob_free") if hh == 0 else None])
            B["t_wst_free"] = tc
            tow.append(tc)
        t_av_ready = [B["t_av_evac%d" % hh] for hh in range(HPG)]
        t_av_ready.append(k.drainV(B["dscr"], t_av_ready[-1]))
        t_op_ev = []
        for tt in range(4):
            for oc in range(8):
                idx = (tt * 8 + oc) % 8
                ps = (ps_a if idx < 4 else ps_b)[0:128, (idx % 4) * 512:((idx % 4) + 1) * 512]
                w0 = tow + t_av_ready + [t_prev_opmm[idx],
                                         B.get("t_psa_free") if idx < 4 else B.get("t_psb_free")]
                tm = None
                for hh in range(HPG):
                    tm = k.mm(ps, wob[0:64, hh * 1024 + oc * 128: hh * 1024 + (oc + 1) * 128],
                              av[0:64, hh * T + tt * 512: hh * T + (tt + 1) * 512],
                              start=(hh == 0), stop=(hh == HPG - 1),
                              waits=(w0 if hh == 0 else []))
                if has_bias and g == 0:
                    b = bob[:, oc:oc + 1]
                    bw = [t_bob]
                else:
                    b = 0.0
                    bw = []
                te = k.stt(X[:, oc * T + tt * 512: oc * T + (tt + 1) * 512],
                           ps, b, X[:, oc * T + tt * 512: oc * T + (tt + 1) * 512],
                           ALU.add, ALU.add, waits=[tm, t_evac_x] + bw)
                t_prev_opmm[idx] = te
                t_op_ev.append(te)
        B["t_psa_free"] = t_op_ev[-1]
        B["t_psb_free"] = t_op_ev[-1]
        B["t_av_free"] = t_op_ev[-1]
        B["t_vt_free"] = t_op_ev[-1]
        B["t_wob_free"] = t_op_ev[-1]
        B["t_wqb_free"] = t_op_ev[-1]
        B["t_wkb_free"] = t_op_ev[-1]
        B["t_wvb_free"] = t_op_ev[-1]
        for key in ("t_psa_q00", "t_psa_q01", "t_psa_q10", "t_psa_q11"):
            B[key] = t_op_ev[-1]
        t_evac_x = t_op_ev[-1]

    B["tX"] = t_evac_x
    B["tXb_free"] = Tok("T", len(k.ops["T"]) - 1)



def emit_ln(k, B, st_scratch):
    mean_sb, m2v, sq_sb = st_scratch
    X, Xb = B["X"], B["Xb"]
    ps_a = B["ps_a"]
    t_out = None
    t_ps_rd = None
    for th in range(2):
        toff = th * 1024
        tmean_mm = None
        tsq_mm = None
        tcasts = []
        for dcx in range(DC):
            xs = X[:, dcx * T + toff: dcx * T + toff + 1024]
            xbs = Xb[:, dcx * T + toff: dcx * T + toff + 1024]
            w0 = [B["tX"]]
            if dcx == 0:
                w0.append(B.get("tXb_free"))
                w0.append(B.get("t_psa_free") if th == 0 else t_ps_rd)
            tc = k.copyV(xbs, xs, waits=w0)
            tcasts.append(tc)
            tq = k.tt(sq_sb[:], xbs, xbs, ALU.mult, waits=[tc, tsq_mm])
            for j in range(2):
                tmean_mm = k.mm(ps_a[:, j * 512:(j + 1) * 512], B["cm"][:],
                                Xb[:, dcx * T + toff + j * 512: dcx * T + toff + (j + 1) * 512],
                                start=(dcx == 0), stop=(dcx == DC - 1),
                                waits=(w0 + [tc]) if (j == 0 and dcx == 0)
                                else ([tc] if j == 0 else []))
            for j in range(2):
                tsq_mm = k.mm(ps_a[:, 1024 + j * 512: 1024 + (j + 1) * 512], B["cm"][:],
                              sq_sb[:, j * 512:(j + 1) * 512],
                              start=(dcx == 0), stop=(dcx == DC - 1),
                              waits=[tq] if j == 0 else [])
        tmean = k.copyV(mean_sb[:], ps_a[:, 0:1024], waits=[tmean_mm])
        tm2 = k.tt(m2v[:], mean_sb[:], mean_sb[:], ALU.mult, waits=[tmean])
        tvar = k.tt(m2v[:], ps_a[:, 1024:2048], m2v[:], ALU.subtract, waits=[tm2, tsq_mm])
        tvard = k.drainV(B["dscr"], tvar)
        tln = k.act(m2v[:], m2v[:], AF.Ln, bias=B["epsc"][:], waits=[tvar, tvard, B["t_consts"]])
        trs = k.act(m2v[:], m2v[:], AF.Exp, scale=-0.5, waits=[tln])
        tlast = None
        for dcx in range(DC):
            xs = X[:, dcx * T + toff: dcx * T + toff + 1024]
            xbs = Xb[:, dcx * T + toff: dcx * T + toff + 1024]
            t1 = k.tt(xs, xs, mean_sb[:], ALU.subtract, waits=[tmean, tcasts[dcx]])
            t2 = k.tt(xs, xs, m2v[:], ALU.mult, waits=[t1, trs])
            t3 = k.copyV(xbs, xs, waits=[t2])
            tlast = t3
        t_out = tlast
        t_ps_rd = tvar
        B["t_psa_free"] = tvar
    t_outd = k.drainV(B["dscr"], t_out)
    B["tX"] = t_outd
    B["tXb"] = t_outd
    B["tXb_free"] = None


def emit_ffn0(k, B, st, W):
    nc = k.nc
    ps_a, ps_b = B["ps_a"], B["ps_b"]
    X, Xb = B["X"], B["Xb"]

    gac = alloc(st, nc, "f0gac", [128, DC * 40], BF16)
    gbbc = alloc(st, nc, "f0gbbc", [128, 8], F32)
    gbst = alloc(st, nc, "f0gbst", [1, 8], F32)
    ballb = alloc(st, nc, "f0ballb", [32, 4096], BF16)
    wib = alloc(st, nc, "f0wib", [128, 32], F32)
    wob = alloc(st, nc, "f0wob", [128, 8], F32)
    lgc = alloc(st, nc, "f0lgc", [128, 40], F32)
    mx = alloc(st, nc, "f0mx", [128, 1], F32)
    msk = alloc(st, nc, "f0msk", [128, 8], F32)
    chat = alloc(st, nc, "f0chat", [128, 32], F32)
    chT = alloc(st, nc, "f0chT", [32, T], BF16)
    wst = [alloc(st, nc, "f0wst0", [128, 1024], F32),
           alloc(st, nc, "f0wst1", [128, 1024], F32)]
    wibt = [alloc(st, nc, "f0wib0", [128, 1024], BF16),
            alloc(st, nc, "f0wib1", [128, 1024], BF16)]
    wost = alloc(st, nc, "f0wost", [128, 2048], F32)
    wocb = [alloc(st, nc, "f0woc0", [128, 4096], BF16),
            alloc(st, nc, "f0woc1", [128, 4096], BF16)]
    inter = alloc(st, nc, "f0inter", [128, 32 * 512], BF16)

    ga_h = k.din("f0ga", W["ga"])
    gb_h = k.din("f0gb", W["gb"])
    ball_h = k.din("f0ball", W["ball"])
    wi_h = k.din("f0wi", W["wi"])
    wib_h = k.din("f0wib_in", W["wib"])
    wo_h = k.din("f0wo", W["wo"])
    wob_h = k.din("f0wob_in", W["wob"])

    td = k.dma(wst[0][:, 0:DC * 40], ga_h[:])
    tgac = k.copyG(gac[:], wst[0][:, 0:DC * 40], waits=[td])
    tprev = tgac
    tball = None
    for q4 in range(4):
        td = k.dma(wst[1][0:32, :], ball_h[0:32, q4 * 1024:(q4 + 1) * 1024],
                   waits=[tball])
        tball = k.copyG(ballb[0:32, q4 * 1024:(q4 + 1) * 1024], wst[1][0:32, :],
                        waits=[td])
    tgb = k.dma(gbst[:], gb_h[:])
    twib = k.dma(wib[:], wib_h[:])
    twob = k.dma(wob[:], wob_h[:])
    tgm = k.mm(ps_b[:, 1024:1032], B["onesrow_f"][0:1, :], gbst[0:1, :],
               start=True, stop=True, waits=[tgb, B["t_consts"], B.get("t_psb_av_free")])
    tgbc = k.copyV(gbbc[:], ps_b[:, 1024:1032], waits=[tgm])

    t_tr_prev = None
    t_lg_prev = None
    tchT = []
    for ts_ in range(16):
        w0 = [B["tXb"], tgac, t_lg_prev]
        tm = None
        for dcx in range(DC):
            tm = k.mm(ps_b[:, 0:40], Xb[:, dcx * T + ts_ * 128: dcx * T + (ts_ + 1) * 128],
                      gac[:, dcx * 40:(dcx + 1) * 40], start=(dcx == 0), stop=(dcx == DC - 1),
                      waits=(w0 if dcx == 0 else []))
        tcp = k.copyV(lgc[:], ps_b[:, 0:40], waits=[tm, tgbc])
        t_lg_prev = tcp
        tga = k.tt(lgc[:, 0:8], lgc[:, 0:8], gbbc[:], ALU.add, waits=[tcp])
        tmx = k.red(mx[:], lgc[:, 0:8], ALU.max, waits=[tga])
        toh = k.ts(msk[:], lgc[:, 0:8], mx[:], None, ALU.is_equal, waits=[tmx])
        tch = []
        for r in range(4):
            tch.append(k.tt(chat[:, r:32:4], lgc[:, 8 + r:40:4], msk[:], ALU.mult,
                            waits=[toh, t_tr_prev]))
        tchd = k.drainV(B["dscr"], tch[-1])
        ttr = k.transp(ps_b[0:32, 512:640], chat[:], B["identf"][:],
                       waits=tch + [tchd, B["t_ident"], t_tr_prev])
        t_tr_prev = ttr
        tcc = k.copyV(chT[0:32, ts_ * 128:(ts_ + 1) * 128], ps_b[0:32, 512:640], waits=[ttr])
        t_tr_prev = tcc
        tchT.append(k.drainV(B["dscr"], tcc))

    if DEBUG_FFN0:
        dbg_cht = nc.dram_tensor("dbg_cht", [32, T], F32, kind="ExternalOutput")
        cht32 = alloc(st, nc, "f0cht32", [32, T], F32)
        tdc = k.copyV(cht32[:], chT[:], waits=[tchT[-1]])
        tdd = k.dma(dbg_cht[:], cht32[:], waits=[tdc, k.drainV(B["dscr"], tdc)])
        k.wait_only("S", [tdd])

    t_wi_cast_free = [None, None]
    t_wi_mm_done = [None, None]
    t_inter_ev = [None] * 32
    t_x_ev = B["tX"]
    t_wocb_mm = [None, None]
    t_slot_ev = [None] * 4
    t_part1_done = t_tr_prev
    for tt in range(4):
        for fc in range(32):
            p = fc % 2
            td = k.dma(wst[p][:], wi_h[fc, :, :],
                       waits=[t_wi_cast_free[p],
                              tprev if (tt == 0 and fc == 0) else None,
                              tball if (tt == 0 and fc == 1) else None])
            if p == 0:
                tc = k.copyG(wibt[p][:], wst[p][:], waits=[td, t_wi_mm_done[p]])
            else:
                tc0 = k.copyV(wibt[p][:], wst[p][:], waits=[td, t_wi_mm_done[p]])
                tc = k.drainV(B["dscr"], tc0)
            t_wi_cast_free[p] = tc
            pa = ps_a[:, p * 512:(p + 1) * 512]
            w0 = [tc, B["tXb"], t_inter_ev[fc] if tt > 0 else B.get("t_psa_free"),
                  t_inter_ev[(fc - 2) % 32] if (tt > 0 or fc >= 2) else None]
            tm = None
            for dcx in range(DC):
                tm = k.mm(pa, wibt[p][:, dcx * 128:(dcx + 1) * 128],
                          Xb[:, dcx * T + tt * 512: dcx * T + (tt + 1) * 512],
                          start=(dcx == 0), stop=False, waits=(w0 if dcx == 0 else []))
            tm = k.mm(pa, ballb[0:32, fc * 128:(fc + 1) * 128],
                      chT[0:32, tt * 512:(tt + 1) * 512], start=False, stop=True,
                      waits=[tball] + (tchT if (tt == 0 and fc == 0) else []))
            t_wi_mm_done[p] = tm
            tev = k.act(inter[:, fc * 512:(fc + 1) * 512], pa, AF.Relu,
                        bias=wib[:, fc:fc + 1], waits=[tm, twib])
            t_inter_ev[fc] = tev
        for oc in range(8):
            p = oc % 2
            tc = None
            for q2 in range(2):
                td = k.dma(wost[:], wo_h[oc, :, q2 * 2048:(q2 + 1) * 2048],
                           waits=[B.get("t_wost_free")])
                if oc % 2 == 0:
                    tc = k.copyG(wocb[p][:, q2 * 2048:(q2 + 1) * 2048], wost[:],
                                 waits=[td, t_wocb_mm[p] if q2 == 0 else None])
                else:
                    tc0 = k.copyV(wocb[p][:, q2 * 2048:(q2 + 1) * 2048], wost[:],
                                  waits=[td, t_wocb_mm[p] if q2 == 0 else None])
                    tc = k.drainV(B["dscr"], tc0)
                B["t_wost_free"] = tc
            slot = oc % 4
            pb = ps_b[:, slot * 512:(slot + 1) * 512]
            w0 = [tc, t_slot_ev[slot],
                  t_part1_done if (tt == 0 and oc == 0) else None,
                  tgbc if (tt == 0 and oc == 0) else None]
            w0 += [t_inter_ev[ic] for ic in range(32)] if oc == 0 else []
            tm = None
            for ic in range(32):
                tm = k.mm(pb, wocb[p][:, ic * 128:(ic + 1) * 128],
                          inter[:, ic * 512:(ic + 1) * 512],
                          start=(ic == 0), stop=(ic == 31), waits=(w0 if ic == 0 else []))
            t_wocb_mm[p] = tm
            tev = k.stt(X[:, oc * T + tt * 512: oc * T + (tt + 1) * 512], pb,
                        wob[:, oc:oc + 1], X[:, oc * T + tt * 512: oc * T + (tt + 1) * 512],
                        ALU.add, ALU.add, waits=[tm, twob, t_x_ev])
            t_slot_ev[slot] = tev
            t_x_ev = tev
    if DEBUG_FFN0:
        dbg_int = nc.dram_tensor("dbg_int", [128, 32 * 512], F32, kind="ExternalOutput")
        int32 = alloc(st, nc, "f0int32", [128, 2048], F32)
        tds = []
        for qq in range(8):
            tdc = k.copyV(int32[:], inter[:, qq * 2048:(qq + 1) * 2048],
                          waits=[t_inter_ev[i] for i in range(32)] + [tds[-1] if tds else None])
            tdd = k.dma(dbg_int[:, qq * 2048:(qq + 1) * 2048], int32[:],
                        waits=[tdc, k.drainV(B["dscr"], tdc)])
            tds.append(tdd)
        k.wait_only("S", [tds[-1]])
    B["tX"] = t_x_ev
    B["t_psa_free"] = t_x_ev
    B["t_psb_av_free"] = t_x_ev
    B["tXb_free"] = Tok("T", len(k.ops["T"]) - 1)


def emit_ffn1(k, B, st_outer, xstack):
    nc = k.nc
    ps_a, ps_b = B["ps_a"], B["ps_b"]
    X, Xb = B["X"], B["Xb"]

    clsb = alloc(st_outer, nc, "f1clsb", [128, DC * 40], BF16)
    clsst = alloc(st_outer, nc, "f1clsst", [128, DC * 40], F32)
    ebst = alloc(st_outer, nc, "f1ebst", [1, 40], F32)
    ebbc = alloc(st_outer, nc, "f1ebbc", [128, 40], F32)
    lgc = alloc(st_outer, nc, "f1lgc", [128, 40], F32)
    nmx = alloc(st_outer, nc, "f1nmx", [128, 1], F32)
    es = alloc(st_outer, nc, "f1es", [128, 40], F32)
    ssum = alloc(st_outer, nc, "f1ssum", [128, 1], F32)
    rs = alloc(st_outer, nc, "f1rs", [128, 1], F32)
    scov = alloc(st_outer, nc, "f1sco", [128, 40], F32)
    sel = alloc(st_outer, nc, "f1sel", [128, 40], F32)
    m6 = alloc(st_outer, nc, "f1m6", [128, 1], F32)
    oh6 = alloc(st_outer, nc, "f1oh6", [128, 40], F32)
    wf = alloc(st_outer, nc, "f1wf", [128, 40], F32)
    zsc = alloc(st_outer, nc, "f1zsc", [128, 1], F32)
    zw1 = alloc(st_outer, nc, "f1zw1", [128, 16], F32)
    wr = alloc(st_outer, nc, "f1wr", [128, 16 * 32], F32)
    xts = [alloc(st_outer, nc, "f1xts0", [128, 1024], F32),
           alloc(st_outer, nc, "f1xts1", [128, 1024], F32)]

    cls_h = k.din("f1cls", k._W1["cls"])
    eb_h = k.din("f1eb", k._W1["eb"])
    wgu_h = k.din("f1wgu", k._W1["wgu"])
    wd_h = k.din("f1wd", k._W1["wd"])
    xtok = nc.dram_tensor("xtok", [T, 1024], F32, kind="Internal")
    y = nc.dram_tensor("y", [T, 1024], F32, kind="ExternalOutput")

    td = k.dma(clsst[:], cls_h[:])
    tcls = k.copyG(clsb[:], clsst[:], waits=[td])
    teb = k.dma(ebst[:], eb_h[:])
    tem = k.mm(ps_b[:, 1024:1064], B["onesrow_f"][0:1, :], ebst[0:1, :],
               start=True, stop=True, waits=[teb, B["t_consts"], B.get("t_psb_av_free")])
    tebc = k.copyV(ebbc[:], ps_b[:, 1024:1064], waits=[tem])

    t_xd = []
    t_rt_prev = None
    t_ps_tr_prev = B.get("t_psa_free")
    t_xts_free = [None, None]
    for s in range(16):
        w0 = [B["tXb"], tcls, t_rt_prev]
        tm = None
        for dcx in range(DC):
            tm = k.mm(ps_b[:, 0:40], Xb[:, dcx * T + s * 128: dcx * T + (s + 1) * 128],
                      clsb[:, dcx * 40:(dcx + 1) * 40], start=(dcx == 0), stop=(dcx == DC - 1),
                      waits=(w0 if dcx == 0 else []))
        tcp = k.copyV(lgc[:], ps_b[:, 0:40], waits=[tm])
        t_rt_prev = tcp
        tnm = k.red(nmx[:], lgc[:], ALU.max, negate=True, waits=[tcp])
        tex = k.act(es[:], lgc[:], AF.Exp, bias=nmx[:], accum=ssum[:], waits=[tnm])
        tdra = k.drainA(B["dscr"], tex)
        trs = k.recip(rs[:], ssum[:], waits=[tex, tdra])
        tsc = k.ts(scov[:], es[:], rs[:], None, ALU.mult, waits=[trs, tex])
        tsb = k.tt(lgc[:], scov[:], ebbc[:], ALU.add, waits=[tsc, tebc])
        tse = k.msetV(sel[:], 0.0, waits=[tsb])
        tprev = tse
        for r6 in range(6):
            tm6 = k.red(m6[:], lgc[:], ALU.max, waits=[tprev])
            th6 = k.ts(oh6[:], lgc[:], m6[:], None, ALU.is_equal, waits=[tm6])
            tsl = k.tt(sel[:], sel[:], oh6[:], ALU.max, waits=[th6])
            tprev = k.stt(lgc[:], oh6[:], -1e9, lgc[:], ALU.mult, ALU.add, waits=[th6, tsl])
        tw = k.stt(wf[:], scov[:], 2.5, sel[:], ALU.mult, ALU.mult, waits=[tprev, tsc])
        tzw = k.red(zsc[:], wf[:, 32:40], ALU.add, waits=[tw])
        k.ts(zw1[:, s:s + 1], zsc[:], 1.0, None, ALU.add, waits=[tzw])
        k.copyV(wr[:, s * 32:(s + 1) * 32], wf[:, 0:32], waits=[tw])
        ttr = None
        for dcx in range(DC):
            ttr = k.transp(ps_a[:, dcx * 128:(dcx + 1) * 128],
                           X[:, dcx * T + s * 128: dcx * T + (s + 1) * 128],
                           B["identf"][:],
                           waits=([B["tX"], B["t_ident"], t_ps_tr_prev] if dcx == 0 else []))
        txc = k.copyV(xts[s % 2][:], ps_a[:, 0:1024], waits=[ttr, t_xts_free[s % 2]])
        t_ps_tr_prev = txc
        tdrx = k.drainV(B["dscr"], txc)
        txd = k.dma(xtok[s * 128:(s + 1) * 128, :], xts[s % 2][:], waits=[txc, tdrx])
        t_xts_free[s % 2] = txd
        t_xd.append(txd)
    t_route_done = Tok("V", len(k.ops["V"]) - 1)

    k.barrier()
    xstack.close()

    stb = ExitStack()
    st_outer.enter_context(stb)
    oacc = alloc(stb, nc, "f1oacc", [128, 16 * 1024], F32)
    wgub = [alloc(stb, nc, "f1wgub0", [128, DC * 1024], BF16),
            alloc(stb, nc, "f1wgub1", [128, DC * 1024], BF16)]
    wdb = [alloc(stb, nc, "f1wdb0", [128, 4 * 1024], BF16),
           alloc(stb, nc, "f1wdb1", [128, 4 * 1024], BF16)]
    wstg = [alloc(stb, nc, "f1wstg0", [128, 1024], F32),
            alloc(stb, nc, "f1wstg1", [128, 1024], F32)]
    sg = [alloc(stb, nc, "f1sg0", [128, 512], BF16),
          alloc(stb, nc, "f1sg1", [128, 512], BF16)]
    iw = [alloc(stb, nc, "f1iw0", [128, 512], F32),
          alloc(stb, nc, "f1iw1", [128, 512], F32)]
    intT = alloc(stb, nc, "f1intT", [128, 512], BF16)
    xf = alloc(stb, nc, "f1xf", [128, 1024], F32)
    xc = alloc(stb, nc, "f1xc", [128, 1024], F32)
    s1 = alloc(stb, nc, "f1s1", [128, 1], F32)
    mn = alloc(stb, nc, "f1mn", [128, 1], F32)
    v2 = alloc(stb, nc, "f1v2", [128, 1], F32)
    vv = alloc(stb, nc, "f1vv", [128, 1], F32)
    osb = [alloc(stb, nc, "f1osb0", [128, 1024], F32),
           alloc(stb, nc, "f1osb1", [128, 1024], F32)]

    t_wfree = [None, None]
    t_stg_free = [None, None]
    t_ev = {}
    t_sg_free = [None, None]
    t_iw_free = [None, None]
    t_gu_free = [None, None]
    t_trslot_free = [tebc, tebc]
    t_int_free = None
    t_y_free = None
    wcasts = {}
    prev = None

    def drain_T_tr(pv):
        ttr = None
        for j in range(4):
            ttr = k.transp(ps_b[:, 1024 + pv["trs"] * 512 + j * 128:
                                1024 + pv["trs"] * 512 + (j + 1) * 128],
                           pv["iw"][:, j * 128:(j + 1) * 128], B["identf"][:],
                           waits=([pv["tiw"], t_trslot_free[pv["trs"]]] if j == 0 else []))
        return ttr

    for e in range(32):
        p = e % 2
        tcasts = []
        for hc in range(DC):
            sp = hc % 2
            td = k.dma(wstg[sp][:], wgu_h[e, :, hc * 1024:(hc + 1) * 1024],
                       waits=[t_stg_free[sp]])
            tc = k.copyG(wgub[p][:, hc * 1024:(hc + 1) * 1024], wstg[sp][:],
                         waits=[td, t_wfree[p] if hc == 0 else None])
            t_stg_free[sp] = tc
            tcasts.append(tc)
        for ic in range(4):
            sp = ic % 2
            td = k.dma(wstg[sp][:], wd_h[e, :, ic * 1024:(ic + 1) * 1024],
                       waits=[t_stg_free[sp]])
            tc = k.copyG(wdb[p][:, ic * 1024:(ic + 1) * 1024], wstg[sp][:],
                         waits=[td, t_wfree[p] if ic == 0 else None])
            t_stg_free[sp] = tc
            tcasts.append(tc)
        wcasts[e] = tcasts
        for s in range(16):
            idx = e * 16 + s
            gsl = idx % 2
            if prev is not None:
                ttr = drain_T_tr(prev)
                t_iw_free[prev["gsl"]] = ttr
                tic = k.copyA(intT[:], ps_b[:, 1024 + prev["trs"] * 512:
                                            1536 + prev["trs"] * 512],
                              waits=[ttr, t_int_free])
                t_trslot_free[prev["trs"]] = tic
            pg = ps_a[:, gsl * 1024:(gsl + 1) * 1024]
            w0 = tcasts + [B["tXb"], t_gu_free[gsl]]
            tm = None
            for hc in range(DC):
                for j in range(2):
                    tm = k.mm(pg[:, j * 512:(j + 1) * 512],
                              Xb[:, hc * T + s * 128: hc * T + (s + 1) * 128],
                              wgub[p][:, hc * 1024 + j * 512: hc * 1024 + (j + 1) * 512],
                              start=(hc == 0), stop=(hc == DC - 1),
                              waits=(w0 if (hc == 0 and j == 0) else []))
            if prev is not None:
                tdn = None
                w1 = [tic, t_y_free, t_ev.get((prev["e"] - 1, prev["s"]))]
                for ic in range(4):
                    for j in range(2):
                        tdn = k.mm(ps_b[:, j * 512:(j + 1) * 512],
                                   intT[:, ic * 128:(ic + 1) * 128],
                                   wdb[prev["p"]][:, ic * 1024 + j * 512:
                                                  ic * 1024 + (j + 1) * 512],
                                   start=(ic == 0), stop=(ic == 3),
                                   waits=(w1 if (ic == 0 and j == 0) else []))
                t_int_free = tdn
                t_wfree[prev["p"]] = tdn
            tsg = k.act(sg[gsl][:], pg[:, 0:512], AF.Silu,
                        waits=[tm, t_sg_free[gsl]])
            if prev is not None:
                pe, ps_ = prev["e"], prev["s"]
                if pe == 0:
                    tev = k.copyV(oacc[:, ps_ * 1024:(ps_ + 1) * 1024],
                                  ps_b[:, 0:1024], waits=[tdn])
                else:
                    tev = k.tt(oacc[:, ps_ * 1024:(ps_ + 1) * 1024], ps_b[:, 0:1024],
                               oacc[:, ps_ * 1024:(ps_ + 1) * 1024], ALU.add,
                               waits=[tdn, t_ev[(pe - 1, ps_)]])
                t_ev[(pe, ps_)] = tev
                t_y_free = tev
            tiw = k.stt(iw[gsl][:], sg[gsl][:], wr[:, s * 32 + e: s * 32 + e + 1],
                        pg[:, 512:1024], ALU.mult, ALU.mult,
                        waits=[tsg, t_route_done, t_iw_free[gsl]])
            t_gu_free[gsl] = tiw
            t_sg_free[gsl] = tiw
            prev = {"e": e, "s": s, "p": p, "gsl": gsl, "trs": idx % 2,
                    "iw": iw[gsl], "tiw": k.drainV(B["dscr"], tiw)}
    ttr = drain_T_tr(prev)
    tic = k.copyA(intT[:], ps_b[:, 1024 + prev["trs"] * 512: 1536 + prev["trs"] * 512],
                  waits=[ttr, t_int_free])
    tdn = None
    w1 = [tic, t_y_free, t_ev.get((prev["e"] - 1, prev["s"]))]
    for ic in range(4):
        for j in range(2):
            tdn = k.mm(ps_b[:, j * 512:(j + 1) * 512],
                       intT[:, ic * 128:(ic + 1) * 128],
                       wdb[prev["p"]][:, ic * 1024 + j * 512: ic * 1024 + (j + 1) * 512],
                       start=(ic == 0), stop=(ic == 3),
                       waits=(w1 if (ic == 0 and j == 0) else []))
    tev = k.tt(oacc[:, prev["s"] * 1024:(prev["s"] + 1) * 1024], ps_b[:, 0:1024],
               oacc[:, prev["s"] * 1024:(prev["s"] + 1) * 1024], ALU.add,
               waits=[tdn, t_ev[(prev["e"] - 1, prev["s"])]])
    t_ev[(prev["e"], prev["s"])] = tev

    t_out_dma = []
    t_xts2_free = [t_xd[-1], t_xd[-1]]
    t_osb_free = [None, None]
    t_prev_sq = None
    for s in range(16):
        tld = k.dma(xts[s % 2][:], xtok[s * 128:(s + 1) * 128, :],
                    waits=[t_xts2_free[s % 2], t_xd[s]])
        tf = k.stt(xf[:], xts[s % 2][:], zw1[:, s:s + 1], oacc[:, s * 1024:(s + 1) * 1024],
                   ALU.mult, ALU.add, waits=[tld, t_ev[(31, s)], t_prev_sq])
        t_xts2_free[s % 2] = tf
        ts1 = k.red(s1[:], xf[:], ALU.add, waits=[tf])
        tmn = k.ts(mn[:], s1[:], 1.0 / 1024.0, None, ALU.mult, waits=[ts1])
        txc = k.ts(xc[:], xf[:], mn[:], None, ALU.subtract, waits=[tmn, t_prev_sq])
        txcd = k.drainV(B["dscr"], txc)
        tsq = k.act(xf[:], xc[:], AF.Square, accum=v2[:], waits=[txc, txcd])
        t_prev_sq = tsq
        tdrq = k.drainA(B["dscr"], tsq)
        tvv = k.ts(vv[:], v2[:], 1.0 / 1024.0, None, ALU.mult, waits=[tsq, tdrq])
        tvvd = k.drainV(B["dscr"], tvv)
        tl = k.act(vv[:], vv[:], AF.Ln, bias=B["epsc"][:], waits=[tvv, tvvd])
        tr2 = k.act(vv[:], vv[:], AF.Exp, scale=-0.5, waits=[tl])
        to = k.ts(osb[s % 2][:], xc[:], vv[:], None, ALU.mult,
                  waits=[tr2, txc, t_osb_free[s % 2]])
        tdro = k.drainV(B["dscr"], to)
        tod = k.dma(y[s * 128:(s + 1) * 128, :], osb[s % 2][:], waits=[to, tdro])
        t_osb_free[s % 2] = tod
        t_out_dma.append(tod)
    k.wait_only("S", [t_out_dma[-1]])



def build_program(A, blocks):
    nc = bass.Bass("TRN2", target_bir_lowering=False)
    k = KB(nc)
    k._W1 = A["f1"]
    B = {}
    with ExitStack() as outer:
        B["ps_a"] = outer.enter_context(nc.psum_tensor("ps_a", [128, 2048], F32))
        B["ps_b"] = outer.enter_context(nc.psum_tensor("ps_b", [128, 2048], F32))
        sems = {e: outer.enter_context(nc.semaphore("sem" + e)) for e in KB.ENGS}
        k.gdr = alloc(outer, nc, "gdr", [128, 4], F32)
        emit_consts(k, B, outer)

        B["Xb"] = alloc(outer, nc, "Xb", [128, DC * T], BF16)

        xstack = ExitStack()
        outer.enter_context(xstack)
        B["X"] = alloc(xstack, nc, "X", [128, DC * T], F32, side="right")

        emit_load_x(k, B, A["x"])

        for bi, bl in enumerate(blocks):
            if bi > 0:
                k.barrier()
            if bl == "attn0" or bl == "attn1":
                W = A["a0"] if bl == "attn0" else A["a1"]
                with ExitStack() as ph:
                    lnscr = (alloc(ph, nc, bl + "lnm", [128, 1024], F32),
                             alloc(ph, nc, bl + "lnv", [128, 1024], F32),
                             alloc(ph, nc, bl + "lnq", [128, 1024], BF16))
                    emit_attn(k, B, ph, W, has_bias=(bl == "attn0"), pre=bl[-1] + "_")
                    emit_ln(k, B, lnscr)
                    k.barrier()
            elif bl == "ffn0":
                with ExitStack() as ph:
                    lnscr = (alloc(ph, nc, bl + "lnm", [128, 1024], F32),
                             alloc(ph, nc, bl + "lnv", [128, 1024], F32),
                             alloc(ph, nc, bl + "lnq", [128, 1024], BF16))
                    emit_ffn0(k, B, ph, A["f0"])
                    emit_ln(k, B, lnscr)
                    k.barrier()
            elif bl == "ffn1":
                emit_ffn1(k, B, outer, xstack)
        if blocks[-1] != "ffn1":
            emit_store_x(k, B)
        k.finalize(sems)
    return nc, k



def _run(nc, k, xarr_percore, n_cores=NCORES):
    in_maps = []
    for c in range(n_cores):
        m = {}
        for name, (kind, arr) in k.host.items():
            if kind == "percore":
                m[name] = np.ascontiguousarray(arr[c])
            else:
                m[name] = arr
        in_maps.append(m)
    res = run_bass_kernel_spmd(nc, in_maps, core_ids=list(range(n_cores)))
    return res.results


def kernel(**inputs):
    A = _arrange(inputs)
    if FUSED:
        nc, k = build_program(A, ["attn0", "ffn0", "attn1", "ffn1"])
        res = _run(nc, k, A["x"])
        return np.stack([r["y"] for r in res], axis=0)
    else:
        xcur = A["x"]
        for blocks in (["attn0"], ["ffn0"], ["attn1"]):
            nc, k = build_program(A, blocks)
            k.host["x"] = ("percore", xcur)
            res = _run(nc, k, xcur)
            xcur = np.stack([r["xout"] for r in res], axis=0)
        nc, k = build_program(A, ["ffn1"])
        k.host["x"] = ("percore", xcur)
        res = _run(nc, k, xcur)
        return np.stack([r["y"] for r in res], axis=0)


if __name__ == "__main__":
    rng = np.random.default_rng(0)
    fake = {}
    import json
    spec = json.load(open("/root/problem/spec.json"))
    for s in spec["input_specs"]:
        fake[s["name"]] = rng.standard_normal(s["shape"]).astype(np.float32)
    A = _arrange(fake)
    for blocks in (["attn0"], ["ffn0"], ["attn1"], ["ffn1"],
                   ["attn0", "ffn0", "attn1", "ffn1"]):
        nc, k = build_program(A, blocks)
        print(blocks, k.n_ops())
